# version 26
# baseline (speedup 1.0000x reference)
import numpy as np
import ml_dtypes
from contextlib import ExitStack

import concourse.bass as bass
import concourse.mybir as mybir
from concourse.bass_utils import run_bass_kernel_spmd

F32 = mybir.dt.float32
BF16 = mybir.dt.bfloat16
I32 = mybir.dt.int32
AF = mybir.ActivationFunctionType
ALU = mybir.AluOpType

NCORE = 8
T = 2048
D = 4096
KC = 32
NC = 32
TH = 2
THW = 1024
EPS = 1e-5
ISQ32 = float(1.0 / np.sqrt(32.0))
NXQ = 64
NMQ = 128
NGS = 40
DO_MASK = True
DO_S1 = True


def _w(eng, sem, v):
    if v > 0:
        eng.wait_ge(sem, v)


def _ap(base, offset, dims):
    return bass.AP(base.tensor, base.offset + offset, dims)


def build_nc():
    nc = bass.Bass("TRN2", target_bir_lowering=False)

    x_d = nc.dram_tensor("x", [T, D], F32, kind="ExternalInput").ap()
    mask_d = nc.dram_tensor("mask", [D, D], I32, kind="ExternalInput").ap()
    wTb_d = nc.dram_tensor("wTb", [NC, 128, D], F32, kind="ExternalInput").ap()
    WTb_d = nc.dram_tensor("WTb", [NC, 128, D], F32, kind="ExternalInput").ap()
    qkh_d = nc.dram_tensor("qkhTb", [5, KC, 128, 128], F32, kind="ExternalInput").ap()
    BmT_d = nc.dram_tensor("BmT", [128, D], F32, kind="ExternalInput").ap()
    biasA_d = nc.dram_tensor("biasA", [128, 32], F32, kind="ExternalInput").ap()
    biasB_d = nc.dram_tensor("biasB", [128, 32], F32, kind="ExternalInput").ap()
    Esel_d = nc.dram_tensor("Esel", [128, D], BF16, kind="ExternalInput").ap()
    sel_d = nc.dram_tensor("selqk", [128, 16], BF16, kind="ExternalInput").ap()
    E16_d = nc.dram_tensor("E16", [8, 128], BF16, kind="ExternalInput").ap()
    id32_d = nc.dram_tensor("id32", [128, 128], F32, kind="ExternalInput").ap()
    idbf_d = nc.dram_tensor("idbf", [128, 128], BF16, kind="ExternalInput").ap()
    out_d = nc.dram_tensor("out", [T, D], F32, kind="ExternalOutput").ap()

    xn_fm = nc.alloc_sbuf_tensor("xn_fm", [128, KC * T], BF16)
    wh_fm = nc.alloc_sbuf_tensor("wh_fm", [128, T], BF16)
    BmT_sb = nc.alloc_sbuf_tensor("BmT_sb", [128, D], BF16)
    bmE_pT = nc.alloc_sbuf_tensor("bmE_pT", [128, D], BF16)
    biassum = nc.alloc_sbuf_tensor("biassum", [128, 32], F32)
    biasB_sb = nc.alloc_sbuf_tensor("biasB_sb", [128, 32], F32)
    sel_sb = nc.alloc_sbuf_tensor("sel_sb", [128, 16], BF16)
    E16_sb = nc.alloc_sbuf_tensor("E16_sb", [8, 128], BF16)
    id32_sb = nc.alloc_sbuf_tensor("id32_sb", [128, 128], F32)
    idbf_sb = nc.alloc_sbuf_tensor("idbf_sb", [128, 128], BF16)
    bmT_sb = nc.alloc_sbuf_tensor("bmT_sb", [128, 128], BF16)
    ST = nc.alloc_sbuf_tensor("ST", [128, 128], F32)

    es = ExitStack()
    S = {}
    for name in [
        "p_ld", "p_bm", "p_e0", "p_e1",
        "l_x0", "l_x1", "l_x2", "l_x3", "l_x4", "l_x5", "l_x6", "l_xq", "l_ssq", "l_var", "l_sq", "l_app",
        "l_xt", "l_xev", "l_f",
        "m_mk0", "m_mk1", "m_mk2", "m_mk3", "m_mk4", "m_mk5", "m_s1", "m_tr", "m_trv",
        "m_bm", "m_bexp", "m_bev",
        "gst0", "gst1", "gst2", "gst3", "gst4", "g_gen0", "g_gen1", "g_gen2", "g_rdy", "g_use", "g_acc", "g_ev",
        "g_qk", "g_sc", "g_scev", "g_str", "g_sub", "g_exp",
        "g_wx", "g_wt", "g_wtv", "g_we", "g_wh",
        "a_wld0", "a_wld1", "a_Wld0", "a_Wld1", "a_wmask", "a_acc", "a_stuff",
        "a_tr", "a_add", "a_xr0", "a_xr1", "a_xr2", "a_xr3", "a_st0", "a_st1", "a_st2", "a_st3",
    ]:
        S[name] = es.enter_context(nc.semaphore(name))

    with nc.Block() as blk:
        @blk.sync
        def _(sync):
            for h in S.values():
                sync.sem_clear(h)

    with nc.Block() as blk:
        @blk.sync
        def _(sync):
            sync.dma_start(biassum.ap(), biasA_d).then_inc(S["p_ld"], 16)
            sync.dma_start(biasB_sb.ap(), biasB_d).then_inc(S["p_ld"], 16)
            sync.dma_start(sel_sb.ap(), sel_d).then_inc(S["p_ld"], 16)
            sync.dma_start(E16_sb.ap(), E16_d).then_inc(S["p_ld"], 16)
            sync.dma_start(id32_sb.ap(), id32_d).then_inc(S["p_ld"], 16)
            sync.dma_start(idbf_sb.ap(), idbf_d).then_inc(S["p_ld"], 16)

        @blk.gpsimd
        def _(g):
            g.dma_start(BmT_sb.ap(), BmT_d).then_inc(S["p_bm"], 16)

    pre = ExitStack()
    S1k = pre.enter_context(nc.sbuf_tensor("S1k", [128, D], BF16))
    mk_st = pre.enter_context(nc.sbuf_tensor("mk_st", [128, 5 * 1024], BF16))
    genring = pre.enter_context(nc.sbuf_tensor("genring", [128, 3 * 512], BF16))
    eselrg = pre.enter_context(nc.sbuf_tensor("eselrg", [128, 2 * 512], BF16))
    stats = pre.enter_context(nc.sbuf_tensor("stats", [128, 32], F32))

    gr32 = genring.ap().bitcast(F32)
    sc_tm = gr32[:, 0:128]
    exp_t = gr32[:, 128:256]
    m_t = gr32[:, 256:272]
    z_t = gr32[:, 272:288]
    zr_t = gr32[:, 288:304]
    wexp_tm = genring.ap()[:, 608:736]

    ln = ExitStack()
    x_t = ln.enter_context(nc.sbuf_tensor("x_t", [128, 6 * 1024], F32))
    xsq = ln.enter_context(nc.sbuf_tensor("xsq", [128, 1024], BF16))
    xn_tm = ln.enter_context(nc.sbuf_tensor("xn_tm", [128, 4 * 1024], BF16))

    prpT = ExitStack()
    tr_ps = prpT.enter_context(nc.psum_tensor("tr_ps", [128, 512], BF16))
    prp = ExitStack()
    XT_ps = [prp.enter_context(nc.psum_tensor("XT_ps%d" % i, [128, 512], BF16))
             for i in range(2)]
    gen_ps = [prp.enter_context(nc.psum_tensor("gen_ps%d" % i, [128, THW], F32))
              for i in range(2)]

    def st(sl, col):
        return stats.ap()[:, sl * 16 + col:sl * 16 + col + 1]

    def emit_strT(t, g8):
        t.wait_ge(S["m_s1"], 16 * (g8 + 1))
        _w(t, S["m_trv"], g8)
        for j in range(4):
            mm = t.transpose(
                tr_ps.ap()[:, j * 128:(j + 1) * 128],
                S1k.ap()[:, (4 * g8 + j) * 128:(4 * g8 + j + 1) * 128],
                idbf_sb.ap(),
            )
        mm.then_inc(S["m_tr"], 1)

    def emit_ST(v, g8):
        v.wait_ge(S["m_tr"], g8 + 1)
        v.tensor_reduce(
            ST.ap()[:, g8 * 16:(g8 + 1) * 16],
            tr_ps.ap().rearrange("p (a b) -> p a b", b=32),
            axis=mybir.AxisListType.X, op=ALU.add,
        ).then_inc(S["m_trv"], 1)

    blkpre = nc.Block().__enter__()

    @blkpre.sync
    def _(sync):
        for xq in range(NXQ):
            _w(sync, S["l_app"], xq - 5)
            ti, qq = xq // 4, xq % 4
            sync.dma_start(
                x_t.ap()[:, (xq % 6) * 1024:(xq % 6 + 1) * 1024],
                x_d[ti * 128:(ti + 1) * 128, qq * 1024:(qq + 1) * 1024],
            ).then_inc(S["l_x%d" % (xq % 6)], 16)
        for g4 in range(2 if DO_MASK else 0):
            sync.dma_start(eselrg.ap()[:, g4 * 512:(g4 + 1) * 512],
                           Esel_d[:, g4 * 512:(g4 + 1) * 512])\
                .then_inc(S["p_e%d" % (g4 % 2)], 16)

    def use_count(s):
        return (s + 1) - (s // 8) - (1 if s % 8 == 7 else 0)

    mq = [0]

    def emit_mask(g, n):
        if not DO_MASK:
            return
        for _ in range(n):
            q = mq[0]
            if q >= NMQ:
                return
            mq[0] += 1
            _w(g, S["m_s1"], q - 4)
            mt, qq = q // 4, q % 4
            g.dma_start(
                mk_st.ap()[:, (q % 5) * 1024:(q % 5 + 1) * 1024],
                mask_d[mt * 128:(mt + 1) * 128, qq * 1024:(qq + 1) * 1024],
                max_dma_last_dim=8192,
            ).then_inc(S["m_mk%d" % (q % 5)], 16)

    @blkpre.gpsimd
    def _(g):
        def emit_slice(u):
            c = u - 3
            if c >= 0:
                if c % 8 == 7:
                    g.wait_ge(S["g_acc"], 2 * (c // 8) + 2)
                else:
                    g.wait_ge(S["g_use"], use_count(c))
            gi, s4 = u // 8, u % 8
            g.dma_start(
                genring.ap()[:, (u % 3) * 512:(u % 3 + 1) * 512],
                _ap(qkh_d, (gi * KC + 4 * s4) * 128 * 128,
                    [[128, 128], [128 * 128, 4], [1, 128]]),
                max_dma_last_dim=8192,
            ).then_inc(S["g_gen%d" % (u % 3)], 16)

        if DO_S1:
            for u in range(3):
                emit_slice(u)
        emit_mask(g, 64)
        for u in (range(3, NGS) if DO_S1 else []):
            emit_slice(u)
            emit_mask(g, 2)
        emit_mask(g, NMQ)

    @blkpre.scalar
    def _(s):
        for ti in range(16):
            sl = ti % 2
            if ti >= 1:
                s.wait_ge(S["l_var"], ti)
                s.activation(st((ti - 1) % 2, 9), st((ti - 1) % 2, 8), AF.Sqrt,
                             bias=0.0, scale=1.0).then_inc(S["l_sq"], 1)
            for qq in range(4):
                xq = 4 * ti + qq
                s.wait_ge(S["l_x%d" % (xq % 6)], 16 * (xq // 6 + 1))
                xa = x_t.ap()[:, (xq % 6) * 1024:(xq % 6 + 1) * 1024]
                _w(s, S["l_xq"], 8 * ti + 2 * qq)
                s.activation(xsq.ap(), xa, AF.Identity, bias=0.0, scale=1.0,
                             accum_out=st(sl, qq)).then_inc(S["l_xq"], 1)
                s.wait_ge(S["l_xq"], 8 * ti + 2 * qq + 1)
                s.activation(xsq.ap(), xa, AF.Square,
                             accum_out=st(sl, 4 + qq)).then_inc(S["l_xq"], 1)
            s.activation(st(sl, 9), st(sl, 0), AF.Identity, bias=0.0,
                         scale=1.0).then_inc(S["l_ssq"], 1)
        s.wait_ge(S["l_var"], 16)
        s.activation(st(15 % 2, 9), st(15 % 2, 8), AF.Sqrt, bias=0.0, scale=1.0)\
            .then_inc(S["l_sq"], 1)

    fc = [0]

    mrq = [0]

    def emit_reduce(v, q):
        v.wait_ge(S["m_mk%d" % (q % 5)], 16 * (q // 5 + 1))
        mt, qq = q // 4, q % 4
        with nc.allow_low_precision(reason="0/1 mask 32-sums <=32 exact in bf16"):
            v.tensor_reduce(
                S1k.ap()[:, mt * 128 + qq * 32:mt * 128 + (qq + 1) * 32],
                mk_st.ap()[:, (q % 5) * 1024:(q % 5 + 1) * 1024]
                    .rearrange("p (a b) -> p a b", b=32),
                axis=mybir.AxisListType.X, op=ALU.add,
            ).then_inc(S["m_s1"], 1)
        mrq[0] += 1

    @blkpre.vector
    def _(v):
        def inc(instr):
            fc[0] += 1
            instr.then_inc(S["l_f"], 1)
            return fc[0]

        def wf(n):
            v.wait_ge(S["l_f"], n)

        for ti in range(16):
            sl = ti % 2
            v.wait_ge(S["l_ssq"], ti + 1)
            inc(v.tensor_tensor(st(sl, 10), st(sl, 0), st(sl, 1), op=ALU.add))
            f1 = inc(v.tensor_tensor(st(sl, 11), st(sl, 2), st(sl, 3), op=ALU.add))
            wf(f1)
            f2 = inc(v.tensor_tensor(st(sl, 12), st(sl, 10), st(sl, 11), op=ALU.add))
            wf(f2)
            inc(v.tensor_tensor(st(sl, 10), st(sl, 4), st(sl, 5), op=ALU.add))
            f3 = inc(v.tensor_tensor(st(sl, 11), st(sl, 6), st(sl, 7), op=ALU.add))
            wf(f3)
            f4 = inc(v.tensor_tensor(st(sl, 13), st(sl, 10), st(sl, 11), op=ALU.add))
            wf(f4)
            inc(v.tensor_scalar(st(sl, 14), st(sl, 12), 1.0 / D, None, op0=ALU.mult))
            f5 = inc(v.tensor_scalar(st(sl, 15), st(sl, 13), 1.0 / D, EPS,
                                     op0=ALU.mult, op1=ALU.add))
            wf(f5)
            f6 = inc(v.tensor_tensor(st(sl, 10), st(sl, 14), st(sl, 14), op=ALU.mult))
            wf(f6)
            v.tensor_tensor(st(sl, 8), st(sl, 15), st(sl, 10), op=ALU.subtract)\
                .then_inc(S["l_var"], 1)
            v.wait_ge(S["l_sq"], ti + 1)
            f7 = inc(v.reciprocal(st(sl, 10), st(sl, 9)))
            wf(f7)
            for qq in range(4):
                u = 4 * ti + qq
                _w(v, S["l_xt"], 2 * u - 6)
                v.tensor_scalar(
                    xn_tm.ap()[:, (u % 4) * 1024:(u % 4 + 1) * 1024],
                    x_t.ap()[:, (u % 6) * 1024:(u % 6 + 1) * 1024],
                    st(sl, 14), st(sl, 10),
                    op0=ALU.subtract, op1=ALU.mult,
                ).then_inc(S["l_app"], 1)
            for qq in range(4):
                for sub in range(2):
                    gix = 2 * (4 * ti + qq) + sub
                    v.wait_ge(S["l_xt"], gix + 1)
                    kc0 = 8 * qq + 4 * sub
                    v.tensor_copy(
                        xn_fm.ap().rearrange("p (kc t) -> p kc t", t=T)
                            [:, kc0:kc0 + 4, ti * 128:(ti + 1) * 128],
                        XT_ps[gix % 2].ap().rearrange("p (a b) -> p a b", a=4),
                    ).then_inc(S["l_xev"], 1)
            if DO_MASK:
                for _ in range(4):
                    emit_reduce(v, mrq[0])
                if ti == 9:
                    emit_ST(v, 0)
                if ti == 14:
                    emit_ST(v, 1)
        for q in range(mrq[0], NMQ if DO_MASK else 0):
            emit_reduce(v, q)

    @blkpre.tensor
    def _(t):
        t.wait_ge(S["p_ld"], 16 * 6)
        t.wait_ge(S["p_bm"], 16)
        for u in range(NXQ):
            ti, qq = u // 4, u % 4
            for sub in range(2):
                gix = 2 * u + sub
                t.wait_ge(S["l_app"], u + 1)
                _w(t, S["l_xev"], gix - 1)
                for j in range(4):
                    mm = t.transpose(
                        XT_ps[gix % 2].ap()[:, j * 128:(j + 1) * 128],
                        xn_tm.ap()[:, (u % 4) * 1024 + (4 * sub + j) * 128:
                                   (u % 4) * 1024 + (4 * sub + j + 1) * 128],
                        idbf_sb.ap(),
                    )
                mm.then_inc(S["l_xt"], 1)
            if DO_MASK and u == 31:
                emit_strT(t, 0)
            if DO_MASK and u == 51:
                emit_strT(t, 1)
        t.wait_ge(S["l_xev"], 2 * NXQ)
        for gi in range(5 if DO_S1 else 0):
            _w(t, S["g_ev"], 2 * gi)
            for kc in range(KC):
                u = gi * 8 + kc // 4
                if kc % 4 == 0:
                    t.wait_ge(S["g_gen%d" % (u % 3)], 16 * (u // 3 + 1))
                for th in range(TH):
                    unit = 2 * gi + th
                    for j in range(2):
                        mm = t.matmul(
                            gen_ps[unit % 2].ap()[:, j * 512:(j + 1) * 512],
                            genring.ap()[:, (u % 3) * 512 + (kc % 4) * 128:
                                         (u % 3) * 512 + (kc % 4 + 1) * 128],
                            xn_fm.ap()[:, kc * T + th * THW + j * 512:
                                       kc * T + th * THW + (j + 1) * 512],
                            start=(kc == 0), stop=(kc == KC - 1),
                        )
                if kc % 4 == 3 and kc != KC - 1:
                    mm.then_inc(S["g_use"], 1)
            mm.then_inc(S["g_acc"], 2)

    ln.close()
    qkh = ExitStack()
    q_fm = qkh.enter_context(nc.sbuf_tensor("q_fm", [128, 2 * T], BF16))
    k_fm = qkh.enter_context(nc.sbuf_tensor("k_fm", [128, 2 * T], BF16))
    h_fm = qkh.enter_context(nc.sbuf_tensor("h_fm", [128, T], BF16))
    scfm_t = k_fm
    gen_dst = [q_fm.ap()[:, 0:T], q_fm.ap()[:, T:2 * T],
               k_fm.ap()[:, 0:T], k_fm.ap()[:, T:2 * T], h_fm.ap()]
    wfmT = q_fm.ap()[0:8, 0:T]

    @blkpre.scalar
    def _(s):
        for gi in range(5 if DO_S1 else 0):
            for th in range(TH):
                unit = 2 * gi + th
                s.wait_ge(S["g_acc"], 2 * gi + 2)
                s.activation(
                    gen_dst[gi][:, th * THW:(th + 1) * THW],
                    gen_ps[unit % 2].ap(),
                    AF.Identity, bias=0.0, scale=1.0,
                ).then_inc(S["g_ev"], 1)

    prp.close()
    pr2 = ExitStack()
    sc_ps = pr2.enter_context(nc.psum_tensor("sc_ps", [8, T], F32))
    st_ps = pr2.enter_context(nc.psum_tensor("st_ps", [128, 128], F32))
    wt_ps = pr2.enter_context(nc.psum_tensor("wt_ps", [8, T], BF16))

    @blkpre.tensor
    def _(t):
        if not DO_S1:
            return
        t.wait_ge(S["g_ev"], 10)
        t.wait_ge(S["g_qk"], 2)
        for ch in range(2):
            for j in range(4):
                mm = t.matmul(
                    sc_ps.ap()[:, j * 512:(j + 1) * 512],
                    sel_sb.ap()[:, ch * 8:(ch + 1) * 8],
                    q_fm.ap()[:, ch * T + j * 512:ch * T + (j + 1) * 512],
                    start=(ch == 0), stop=(ch == 1),
                )
        mm.then_inc(S["g_sc"], 1)
        if DO_MASK:
            emit_strT(t, 2)
            emit_strT(t, 3)
        t.wait_ge(S["g_scev"], 1)
        for tg in range(16):
            mm = t.transpose(
                st_ps.ap()[:, tg * 8:(tg + 1) * 8],
                scfm_t.ap().bitcast(F32)[0:8, 0:T][:, tg * 128:(tg + 1) * 128],
                id32_sb.ap()[0:8, 0:8],
            )
        mm.then_inc(S["g_str"], 1)
        if DO_MASK:
            for g8 in range(4, 8):
                emit_strT(t, g8)
        t.wait_ge(S["g_wx"], 1)
        for tg in range(16):
            mm = t.transpose(
                wt_ps.ap()[:, tg * 128:(tg + 1) * 128],
                wexp_tm[:, tg * 8:(tg + 1) * 8],
                idbf_sb.ap(),
            )
        mm.then_inc(S["g_wt"], 1)

    @blkpre.scalar
    def _(s):
        if not DO_S1:
            return
        s.wait_ge(S["g_sc"], 1)
        s.activation(scfm_t.ap().bitcast(F32)[0:8, 0:T], sc_ps.ap(), AF.Identity, bias=0.0, scale=1.0)\
            .then_inc(S["g_scev"], 1)
        s.wait_ge(S["g_sub"], 1)
        s.activation(exp_t, sc_tm, AF.Exp, bias=0.0, scale=ISQ32)\
            .then_inc(S["g_exp"], 1)

    @blkpre.vector
    def _(v):
        def inc(instr):
            fc[0] += 1
            instr.then_inc(S["l_f"], 1)
            return fc[0]

        def wf(n):
            v.wait_ge(S["l_f"], n)

        if not DO_S1:
            return
        v.wait_ge(S["g_ev"], 8)
        for ch in range(2):
            v.tensor_tensor(
                q_fm.ap()[:, ch * T:(ch + 1) * T],
                q_fm.ap()[:, ch * T:(ch + 1) * T],
                k_fm.ap()[:, ch * T:(ch + 1) * T],
                op=ALU.mult,
            ).then_inc(S["g_qk"], 1)
        if DO_MASK:
            emit_ST(v, 2)
            emit_ST(v, 3)
        v.wait_ge(S["g_str"], 1)
        g1 = inc(v.tensor_copy(sc_tm, st_ps.ap()))
        wf(g1)
        g2 = inc(v.tensor_reduce(m_t, sc_tm.rearrange("p (a b) -> p a b", b=8),
                                 axis=mybir.AxisListType.X, op=ALU.max))
        wf(g2)
        v.tensor_tensor(
            sc_tm.rearrange("p (a b) -> p a b", b=8),
            sc_tm.rearrange("p (a b) -> p a b", b=8),
            m_t.broadcast_to((128, 16, 8)), op=ALU.subtract,
        ).then_inc(S["g_sub"], 1)
        if DO_MASK:
            emit_ST(v, 4)
            emit_ST(v, 5)
        v.wait_ge(S["g_exp"], 1)
        g3 = inc(v.tensor_reduce(z_t, exp_t.rearrange("p (a b) -> p a b", b=8),
                                 axis=mybir.AxisListType.X, op=ALU.add))
        wf(g3)
        g4 = inc(v.reciprocal(zr_t, z_t))
        wf(g4)
        if DO_MASK:
            emit_ST(v, 6)
            emit_ST(v, 7)
        v.tensor_tensor(
            wexp_tm.rearrange("p (a b) -> p a b", b=8),
            exp_t.rearrange("p (a b) -> p a b", b=8),
            zr_t.broadcast_to((128, 16, 8)), op=ALU.mult,
        ).then_inc(S["g_wx"], 1)
        v.wait_ge(S["g_wt"], 1)
        v.tensor_copy(wfmT, wt_ps.ap()).then_inc(S["g_wtv"], 1)

    pr2.close()
    pr3 = ExitStack()
    e_ps = pr3.enter_context(nc.psum_tensor("e_ps", [128, THW], F32))
    we_ps = pr3.enter_context(nc.psum_tensor("we_ps", [128, T], F32))

    def emit_e16(t):
        for j in range(4):
            mm = t.matmul(
                we_ps.ap()[:, j * 512:(j + 1) * 512],
                E16_sb.ap(),
                wfmT[:, j * 512:(j + 1) * 512],
                start=True, stop=True,
            )
        mm.then_inc(S["g_we"], 1)

    @blkpre.tensor
    def _(t):
        if DO_S1:
            t.wait_ge(S["g_wtv"], 1)
            emit_e16(t)
        if not DO_MASK:
            return
        t.wait_ge(S["m_bm"], 1)
        t.wait_ge(S["g_ev"], 10)
        if DO_S1:
            t.wait_ge(S["g_wtv"], 1)
        for g8 in range(KC // 8):
            t.wait_ge(S["p_e0"], 16 * (g8 + 1))
            t.wait_ge(S["p_e1"], 16 * (g8 + 1))
            _w(t, S["m_bev"], g8)
            for j in range(8):
                kc = 8 * g8 + j
                mm = t.matmul(
                    e_ps.ap()[:, j * 128:(j + 1) * 128],
                    eselrg.ap()[:, (j // 4) * 512 + (j % 4) * 128:
                                (j // 4) * 512 + (j % 4 + 1) * 128],
                    bmT_sb.ap(),
                    start=True, stop=True,
                )
            mm.then_inc(S["m_bexp"], 1)

    @blkpre.vector
    def _(v):
        def inc(instr):
            fc[0] += 1
            instr.then_inc(S["l_f"], 1)
            return fc[0]

        def wf(n):
            v.wait_ge(S["l_f"], n)

        if DO_S1:
            v.wait_ge(S["g_we"], 1)
            v.wait_ge(S["g_ev"], 10)
        for j in range(4 if DO_S1 else 0):
            v.tensor_tensor(
                wh_fm.ap()[:, j * 512:(j + 1) * 512],
                h_fm.ap()[:, j * 512:(j + 1) * 512],
                we_ps.ap()[:, j * 512:(j + 1) * 512],
                op=ALU.mult,
            ).then_inc(S["g_wh"], 1)
        if DO_MASK:
            v.wait_ge(S["m_trv"], 8)
            fz = inc(v.tensor_scalar(bmT_sb.ap(), ST.ap(), 0.0, None, op0=ALU.is_gt))
            wf(fz)
            v.nop().then_inc(S["m_bm"], 1)
        for g8 in range(KC // 8 if DO_MASK else 0):
            v.wait_ge(S["m_bexp"], g8 + 1)
            v.tensor_copy(
                bmE_pT.ap()[:, g8 * 1024:(g8 + 1) * 1024],
                e_ps.ap(),
            ).then_inc(S["m_bev"], 1)
        v.wait_ge(S["p_ld"], 16 * 6)
        v.tensor_tensor(biassum.ap(), biassum.ap(), biasB_sb.ap(), op=ALU.add)

    @blkpre.sync
    def _(sync):
        for k in range(2, 8 if DO_MASK else 2):
            _w(sync, S["m_bexp"], k // 2)
            sync.dma_start(eselrg.ap()[:, (k % 2) * 512:(k % 2 + 1) * 512],
                           Esel_d[:, k * 512:(k + 1) * 512])\
                .then_inc(S["p_e%d" % (k % 2)], 16)

    blkpre.__exit__(None, None, None)
    pr3.close()
    prpT.close()
    qkh.close()
    pre.close()

    mn = ExitStack()
    wm_ring = mn.enter_context(nc.sbuf_tensor("wm_ring", [128, 2 * D], BF16))
    WT_ring = mn.enter_context(nc.sbuf_tensor("WT_ring", [128, 2 * D], BF16))
    x_res = mn.enter_context(nc.sbuf_tensor("x_res", [128, 4 * THW], F32))
    stuff = mn.enter_context(nc.sbuf_tensor("stuff", [128, 2 * THW], F32))
    mnp = ExitStack()
    ACC0 = mnp.enter_context(nc.psum_tensor("ACC0", [128, THW], F32))
    ACC1 = mnp.enter_context(nc.psum_tensor("ACC1", [128, THW], F32))
    T_ps0 = mnp.enter_context(nc.psum_tensor("T_ps0", [128, THW], F32))
    T_ps1 = mnp.enter_context(nc.psum_tensor("T_ps1", [128, THW], F32))
    ACC = [ACC0, ACC1]
    T_ps = [T_ps0, T_ps1]

    def x_res_ap(th, nc_i):
        return _ap(x_d, (th * THW) * D + nc_i * 128,
                   [[D, 128], [128 * D, 8], [1, 128]])

    def out_ap(th, nc_i, j):
        return _ap(out_d, (th * THW + j * 512) * D + nc_i * 128,
                   [[D, 128], [128 * D, 4], [1, 128]])

    def emit_tr(t, m):
        for th in range(TH):
            t.wait_ge(S["a_stuff"], 2 * m + th + 1)
            _w(t, S["a_add"], (4 * (m - 1) + 2 * th + 2) if m >= 1 else 0)
            for j8 in range(8):
                mm = t.transpose(
                    T_ps[th].ap()[:, j8 * 128:(j8 + 1) * 128],
                    stuff.ap()[:, th * THW + j8 * 128:th * THW + (j8 + 1) * 128],
                    id32_sb.ap(),
                )
            mm.then_inc(S["a_tr"], 1)

    with nc.Block() as blk:
        @blk.gpsimd
        def _(g):
            for nc_i in range(NC):
                _w(g, S["a_acc"], 2 * (nc_i - 1))
                g.dma_start(
                    wm_ring.ap()[:, (nc_i % 2) * D:(nc_i % 2 + 1) * D],
                    _ap(wTb_d, nc_i * 128 * D, [[D, 128], [1, D]]),
                    max_dma_last_dim=8192,
                ).then_inc(S["a_wld%d" % (nc_i % 2)], 16)
                g.dma_start(
                    WT_ring.ap()[:, (nc_i % 2) * D:(nc_i % 2 + 1) * D],
                    _ap(WTb_d, nc_i * 128 * D, [[D, 128], [1, D]]),
                    max_dma_last_dim=8192,
                ).then_inc(S["a_Wld%d" % (nc_i % 2)], 16)

        @blk.vector
        def _(v):
            for i in range(NC + 1):
                if i < NC:
                    v.wait_ge(S["a_wld%d" % (i % 2)], 16 * (i // 2 + 1))
                    w_view = wm_ring.ap()[:, (i % 2) * D:(i % 2 + 1) * D]\
                        .rearrange("p (kc q r) -> p kc q r", kc=KC, q=4)
                    v.tensor_tensor(
                        w_view, w_view,
                        bmE_pT.ap().rearrange("p (kc nb) -> p kc nb", nb=128)
                            [:, :, 4 * i:4 * i + 4].broadcast_to((128, KC, 4, 32)),
                        op=ALU.mult,
                    ).then_inc(S["a_wmask"], 1)
                if i >= 1:
                    nc_i = i - 1
                    for th in range(TH):
                        for j in range(2):
                            xr = x_res.ap()[:, (nc_i % 2) * 2 * THW + th * THW + j * 512:
                                            (nc_i % 2) * 2 * THW + th * THW + (j + 1) * 512]
                            v.wait_ge(S["a_tr"], 2 * nc_i + th + 1)
                            v.wait_ge(S["a_xr%d" % (2 * th + nc_i % 2)],
                                      16 * (nc_i // 2 + 1))
                            v.tensor_tensor(
                                xr,
                                T_ps[th].ap()[:, j * 512:(j + 1) * 512],
                                xr,
                                op=ALU.add,
                            ).then_inc(S["a_add"], 1)

        @blk.tensor
        def _(t):
            for nc_i in range(NC):
                if nc_i >= 1:
                    emit_tr(t, nc_i - 1)
                t.wait_ge(S["a_wmask"], nc_i + 1)
                t.wait_ge(S["a_Wld%d" % (nc_i % 2)], 16 * (nc_i // 2 + 1))
                sl = nc_i % 2
                for th in range(TH):
                    _w(t, S["a_stuff"], 2 * (nc_i - 1) + th + 1)
                    for kc in range(KC):
                        for ring in (wm_ring, WT_ring):
                            for j in range(2):
                                mm = t.matmul(
                                    ACC[th].ap()[:, j * 512:(j + 1) * 512],
                                    ring.ap()[:, sl * D + kc * 128:sl * D + (kc + 1) * 128],
                                    xn_fm.ap()[:, kc * T + th * THW + j * 512:
                                               kc * T + th * THW + (j + 1) * 512],
                                    start=(kc == 0 and ring is wm_ring),
                                    stop=False,
                                )
                    for j in range(2):
                        mm = t.matmul(
                            ACC[th].ap()[:, j * 512:(j + 1) * 512],
                            BmT_sb.ap()[:, nc_i * 128:(nc_i + 1) * 128],
                            wh_fm.ap()[:, th * THW + j * 512:th * THW + (j + 1) * 512],
                            start=False, stop=True,
                        )
                    mm.then_inc(S["a_acc"], 1)
            emit_tr(t, NC - 1)

        @blk.scalar
        def _(s):
            for nc_i in range(NC):
                for th in range(TH):
                    s.wait_ge(S["a_acc"], 2 * nc_i + th + 1)
                    _w(s, S["a_tr"], 2 * (nc_i - 1) + th + 1)
                    s.activation(
                        stuff.ap()[:, th * THW:(th + 1) * THW],
                        ACC[th].ap(),
                        AF.Identity,
                        bias=biassum.ap()[:, nc_i:nc_i + 1],
                        scale=1.0,
                    ).then_inc(S["a_stuff"], 1)

        @blk.sync
        def _(sync):
            for i in range(NC + 1):
                if i < NC:
                    for th in range(TH):
                        if i >= 2:
                            sync.wait_ge(S["a_st%d" % (2 * th)], 16 * (i - 1))
                            sync.wait_ge(S["a_st%d" % (2 * th + 1)], 16 * (i - 1))
                        sync.dma_start(
                            x_res.ap()[:, (i % 2) * 2 * THW + th * THW:
                                       (i % 2) * 2 * THW + (th + 1) * THW],
                            x_res_ap(th, i),
                        ).then_inc(S["a_xr%d" % (2 * th + i % 2)], 16)
                if i >= 1:
                    nc_i = i - 1
                    for th in range(TH):
                        for j in range(2):
                            gidx = 4 * nc_i + 2 * th + j
                            sj = 2 * th + j
                            sync.wait_ge(S["a_add"], gidx + 1)
                            sync.dma_start(
                                out_ap(th, nc_i, j),
                                x_res.ap()[:, (nc_i % 2) * 2 * THW + th * THW + j * 512:
                                           (nc_i % 2) * 2 * THW + th * THW + (j + 1) * 512],
                            ).then_inc(S["a_st%d" % sj], 16)
            for sj in range(4):
                sync.wait_ge(S["a_st%d" % sj], 16 * 32)

    mnp.close()
    mn.close()
    es.close()
    return nc


def _host_constants():
    p = np.arange(128)
    blk = p // 32
    E3 = np.zeros((128, 32, 128), np.float32)
    ET3 = np.zeros((128, 32, 128), np.float32)
    for kc in range(32):
        E3[4 * kc + blk, kc, p] = 1.0
        ET3[p, kc, 4 * kc + blk] = 1.0
    Esel = E3.reshape(128, 4096).astype(ml_dtypes.bfloat16)
    ETsel = ET3.reshape(128, 4096).astype(ml_dtypes.bfloat16)
    sel = np.zeros((128, 16), np.float32)
    for ch in range(2):
        sel[p, ch * 8 + ch * 4 + blk] = 1.0
    selqk = sel.astype(ml_dtypes.bfloat16)
    E16 = np.zeros((8, 128), np.float32)
    E16[np.arange(128) // 16, np.arange(128)] = 1.0
    E16 = E16.astype(ml_dtypes.bfloat16)
    id32 = np.eye(128, dtype=np.float32)
    idbf = np.eye(128).astype(ml_dtypes.bfloat16)
    return Esel, ETsel, selqk, E16, id32, idbf


def kernel(L0_attn_data, L0_attn_mask, L0_attn_weight, L0_attn_bias,
           L0_ffn_W, L0_ffn_b, L0_ffn_Wq, L0_ffn_Wk, L0_ffn_A, L0_ffn_Bm):
    x_full = np.ascontiguousarray(np.asarray(L0_attn_data, np.float32)).reshape(16384, 4096)
    mask = np.ascontiguousarray(np.asarray(L0_attn_mask, np.int32))
    w = np.asarray(L0_attn_weight, np.float32)
    W = np.asarray(L0_ffn_W, np.float32)

    def blocked_T(m):
        return np.ascontiguousarray(
            m.reshape(32, 128, 32, 128).transpose(0, 3, 2, 1).reshape(32, 128, 4096))

    wTb = blocked_T(w)
    WTb = blocked_T(W)

    Wq = np.asarray(L0_ffn_Wq, np.float32)
    Wk = np.asarray(L0_ffn_Wk, np.float32)
    A = np.asarray(L0_ffn_A, np.float32).reshape(128, 4096)
    qkh = np.concatenate([Wq.T, Wk.T, A.T], axis=1)
    qkhTb = np.ascontiguousarray(
        qkh.reshape(32, 128, 5, 128).transpose(2, 0, 1, 3))

    BmT = np.ascontiguousarray(
        np.asarray(L0_ffn_Bm, np.float32).transpose(0, 2, 1).reshape(128, 4096))
    biasA = np.ascontiguousarray(np.asarray(L0_attn_bias, np.float32).reshape(32, 128).T)
    biasB = np.ascontiguousarray(np.asarray(L0_ffn_b, np.float32).reshape(32, 128).T)

    Esel, _ETsel_unused, selqk, E16, id32, idbf = _host_constants()

    nc = build_nc()
    shared = dict(mask=mask, wTb=wTb, WTb=WTb, qkhTb=qkhTb, BmT=BmT,
                  biasA=biasA, biasB=biasB, Esel=Esel,
                  selqk=selqk, E16=E16, id32=id32, idbf=idbf)
    in_maps = [dict(x=np.ascontiguousarray(x_full[c * 2048:(c + 1) * 2048]), **shared)
               for c in range(NCORE)]
    res = run_bass_kernel_spmd(nc, in_maps, core_ids=list(range(NCORE)))
    out = np.concatenate([np.asarray(res.results[c]["out"]) for c in range(NCORE)], axis=0)
    return out.reshape(8, 2048, 4096)


# version 27
# speedup vs baseline: 1.0014x; 1.0014x over previous
import numpy as np
import ml_dtypes
from contextlib import ExitStack

import concourse.bass as bass
import concourse.mybir as mybir
from concourse.bass_utils import run_bass_kernel_spmd

F32 = mybir.dt.float32
BF16 = mybir.dt.bfloat16
I32 = mybir.dt.int32
AF = mybir.ActivationFunctionType
ALU = mybir.AluOpType

NCORE = 8
T = 2048
D = 4096
KC = 32
NC = 32
TH = 2
THW = 1024
EPS = 1e-5
ISQ32 = float(1.0 / np.sqrt(32.0))
NXQ = 64
NMQ = 128
NGS = 40
DO_MASK = True
DO_S1 = True


def _w(eng, sem, v):
    if v > 0:
        eng.wait_ge(sem, v)


def _ap(base, offset, dims):
    return bass.AP(base.tensor, base.offset + offset, dims)


def build_nc():
    nc = bass.Bass("TRN2", target_bir_lowering=False)

    x_d = nc.dram_tensor("x", [T, D], F32, kind="ExternalInput").ap()
    mask_d = nc.dram_tensor("mask", [D, D], I32, kind="ExternalInput").ap()
    wTb_d = nc.dram_tensor("wTb", [NC, 128, D], F32, kind="ExternalInput").ap()
    WTb_d = nc.dram_tensor("WTb", [NC, 128, D], F32, kind="ExternalInput").ap()
    qkh_d = nc.dram_tensor("qkhTb", [5, KC, 128, 128], F32, kind="ExternalInput").ap()
    BmT_d = nc.dram_tensor("BmT", [128, D], F32, kind="ExternalInput").ap()
    biasA_d = nc.dram_tensor("biasA", [128, 32], F32, kind="ExternalInput").ap()
    biasB_d = nc.dram_tensor("biasB", [128, 32], F32, kind="ExternalInput").ap()
    Esel_d = nc.dram_tensor("Esel", [128, D], BF16, kind="ExternalInput").ap()
    sel_d = nc.dram_tensor("selqk", [128, 16], BF16, kind="ExternalInput").ap()
    E16_d = nc.dram_tensor("E16", [8, 128], BF16, kind="ExternalInput").ap()
    id32_d = nc.dram_tensor("id32", [128, 128], F32, kind="ExternalInput").ap()
    idbf_d = nc.dram_tensor("idbf", [128, 128], BF16, kind="ExternalInput").ap()
    out_d = nc.dram_tensor("out", [T, D], F32, kind="ExternalOutput").ap()

    xn_fm = nc.alloc_sbuf_tensor("xn_fm", [128, KC * T], BF16)
    wh_fm = nc.alloc_sbuf_tensor("wh_fm", [128, T], BF16)
    BmT_sb = nc.alloc_sbuf_tensor("BmT_sb", [128, D], BF16)
    bmE_pT = nc.alloc_sbuf_tensor("bmE_pT", [128, D], BF16)
    biassum = nc.alloc_sbuf_tensor("biassum", [128, 32], F32)
    biasB_sb = nc.alloc_sbuf_tensor("biasB_sb", [128, 32], F32)
    sel_sb = nc.alloc_sbuf_tensor("sel_sb", [128, 16], BF16)
    E16_sb = nc.alloc_sbuf_tensor("E16_sb", [8, 128], BF16)
    id32_sb = nc.alloc_sbuf_tensor("id32_sb", [128, 128], F32)
    idbf_sb = nc.alloc_sbuf_tensor("idbf_sb", [128, 128], BF16)
    bmT_sb = nc.alloc_sbuf_tensor("bmT_sb", [128, 128], BF16)
    ST = nc.alloc_sbuf_tensor("ST", [128, 128], F32)

    es = ExitStack()
    S = {}
    for name in [
        "p_ld", "p_bm", "p_e0", "p_e1", "p_e2", "p_e3",
        "l_x0", "l_x1", "l_x2", "l_x3", "l_x4", "l_x5", "l_x6", "l_xq", "l_ssq", "l_var", "l_sq", "l_app",
        "l_xt", "l_xev", "l_f",
        "m_mk0", "m_mk1", "m_mk2", "m_mk3", "m_mk4", "m_mk5", "m_s1", "m_tr", "m_trv",
        "m_bm", "m_bexp", "m_bev",
        "gst0", "gst1", "gst2", "gst3", "gst4", "g_rdy", "g_use", "g_acc", "g_ev",
        "g_qk", "g_sc", "g_scev", "g_str", "g_sub", "g_exp",
        "g_wx", "g_wt", "g_wtv", "g_we", "g_wh",
        "a_wld0", "a_wld1", "a_Wld0", "a_Wld1", "a_wmask", "a_acc", "a_stuff",
        "a_tr", "a_add", "a_xr0", "a_xr1", "a_xr2", "a_xr3", "a_st0", "a_st1", "a_st2", "a_st3",
    ]:
        S[name] = es.enter_context(nc.semaphore(name))

    with nc.Block() as blk:
        @blk.sync
        def _(sync):
            for h in S.values():
                sync.sem_clear(h)

    with nc.Block() as blk:
        @blk.sync
        def _(sync):
            sync.dma_start(biassum.ap(), biasA_d).then_inc(S["p_ld"], 16)
            sync.dma_start(biasB_sb.ap(), biasB_d).then_inc(S["p_ld"], 16)
            sync.dma_start(sel_sb.ap(), sel_d).then_inc(S["p_ld"], 16)
            sync.dma_start(E16_sb.ap(), E16_d).then_inc(S["p_ld"], 16)
            sync.dma_start(id32_sb.ap(), id32_d).then_inc(S["p_ld"], 16)
            sync.dma_start(idbf_sb.ap(), idbf_d).then_inc(S["p_ld"], 16)

        @blk.gpsimd
        def _(g):
            g.dma_start(BmT_sb.ap(), BmT_d).then_inc(S["p_bm"], 16)

    pre = ExitStack()
    S1k = pre.enter_context(nc.sbuf_tensor("S1k", [128, D], BF16))
    mk_st = pre.enter_context(nc.sbuf_tensor("mk_st", [128, 5 * 1024], BF16))
    genring = pre.enter_context(nc.sbuf_tensor("genring", [128, 3 * 512], BF16))
    stats = pre.enter_context(nc.sbuf_tensor("stats", [128, 32], F32))

    gr32 = genring.ap().bitcast(F32)
    sc_tm = gr32[:, 0:128]
    exp_t = gr32[:, 128:256]
    m_t = gr32[:, 256:272]
    z_t = gr32[:, 272:288]
    zr_t = gr32[:, 288:304]
    wexp_tm = genring.ap()[:, 608:736]

    ln = ExitStack()
    x_t = ln.enter_context(nc.sbuf_tensor("x_t", [128, 6 * 1024], F32))
    xsq = ln.enter_context(nc.sbuf_tensor("xsq", [128, 1024], BF16))
    xn_tm = ln.enter_context(nc.sbuf_tensor("xn_tm", [128, 4 * 1024], BF16))

    prpT = ExitStack()
    tr_ps = prpT.enter_context(nc.psum_tensor("tr_ps", [128, 512], BF16))
    prp = ExitStack()
    XT_ps = [prp.enter_context(nc.psum_tensor("XT_ps%d" % i, [128, 512], BF16))
             for i in range(2)]
    gen_ps = [prp.enter_context(nc.psum_tensor("gen_ps%d" % i, [128, THW], F32))
              for i in range(2)]

    def st(sl, col):
        return stats.ap()[:, sl * 16 + col:sl * 16 + col + 1]

    def emit_strT(t, g8):
        t.wait_ge(S["m_s1"], 16 * (g8 + 1))
        _w(t, S["m_trv"], g8)
        for j in range(4):
            mm = t.transpose(
                tr_ps.ap()[:, j * 128:(j + 1) * 128],
                S1k.ap()[:, (4 * g8 + j) * 128:(4 * g8 + j + 1) * 128],
                idbf_sb.ap(),
            )
        mm.then_inc(S["m_tr"], 1)

    def emit_ST(v, g8):
        v.wait_ge(S["m_tr"], g8 + 1)
        v.tensor_reduce(
            ST.ap()[:, g8 * 16:(g8 + 1) * 16],
            tr_ps.ap().rearrange("p (a b) -> p a b", b=32),
            axis=mybir.AxisListType.X, op=ALU.add,
        ).then_inc(S["m_trv"], 1)

    blkpre = nc.Block().__enter__()

    @blkpre.sync
    def _(sync):
        for xq in range(NXQ):
            _w(sync, S["l_app"], xq - 5)
            ti, qq = xq // 4, xq % 4
            sync.dma_start(
                x_t.ap()[:, (xq % 6) * 1024:(xq % 6 + 1) * 1024],
                x_d[ti * 128:(ti + 1) * 128, qq * 1024:(qq + 1) * 1024],
            ).then_inc(S["l_x%d" % (xq % 6)], 16)


    def use_count(s):
        return (s + 1) - (s // 8) - (1 if s % 8 == 7 else 0)

    mq = [0]

    def emit_mask(g, n):
        if not DO_MASK:
            return
        for _ in range(n):
            q = mq[0]
            if q >= NMQ:
                return
            mq[0] += 1
            _w(g, S["m_s1"], q - 4)
            mt, qq = q // 4, q % 4
            g.dma_start(
                mk_st.ap()[:, (q % 5) * 1024:(q % 5 + 1) * 1024],
                mask_d[mt * 128:(mt + 1) * 128, qq * 1024:(qq + 1) * 1024],
                max_dma_last_dim=8192,
            ).then_inc(S["m_mk%d" % (q % 5)], 16)

    @blkpre.gpsimd
    def _(g):
        emit_mask(g, NMQ)

    @blkpre.scalar
    def _(s):
        for ti in range(16):
            sl = ti % 2
            if ti >= 1:
                s.wait_ge(S["l_var"], ti)
                s.activation(st((ti - 1) % 2, 9), st((ti - 1) % 2, 8), AF.Sqrt,
                             bias=0.0, scale=1.0).then_inc(S["l_sq"], 1)
            for qq in range(4):
                xq = 4 * ti + qq
                s.wait_ge(S["l_x%d" % (xq % 6)], 16 * (xq // 6 + 1))
                xa = x_t.ap()[:, (xq % 6) * 1024:(xq % 6 + 1) * 1024]
                _w(s, S["l_xq"], 8 * ti + 2 * qq)
                s.activation(xsq.ap(), xa, AF.Identity, bias=0.0, scale=1.0,
                             accum_out=st(sl, qq)).then_inc(S["l_xq"], 1)
                s.wait_ge(S["l_xq"], 8 * ti + 2 * qq + 1)
                s.activation(xsq.ap(), xa, AF.Square,
                             accum_out=st(sl, 4 + qq)).then_inc(S["l_xq"], 1)
            s.activation(st(sl, 9), st(sl, 0), AF.Identity, bias=0.0,
                         scale=1.0).then_inc(S["l_ssq"], 1)
        s.wait_ge(S["l_var"], 16)
        s.activation(st(15 % 2, 9), st(15 % 2, 8), AF.Sqrt, bias=0.0, scale=1.0)\
            .then_inc(S["l_sq"], 1)

    fc = [0]

    mrq = [0]

    def emit_reduce(v, q):
        v.wait_ge(S["m_mk%d" % (q % 5)], 16 * (q // 5 + 1))
        mt, qq = q // 4, q % 4
        with nc.allow_low_precision(reason="0/1 mask 32-sums <=32 exact in bf16"):
            v.tensor_reduce(
                S1k.ap()[:, mt * 128 + qq * 32:mt * 128 + (qq + 1) * 32],
                mk_st.ap()[:, (q % 5) * 1024:(q % 5 + 1) * 1024]
                    .rearrange("p (a b) -> p a b", b=32),
                axis=mybir.AxisListType.X, op=ALU.add,
            ).then_inc(S["m_s1"], 1)
        mrq[0] += 1

    @blkpre.vector
    def _(v):
        def inc(instr):
            fc[0] += 1
            instr.then_inc(S["l_f"], 1)
            return fc[0]

        def wf(n):
            v.wait_ge(S["l_f"], n)

        for ti in range(16):
            sl = ti % 2
            v.wait_ge(S["l_ssq"], ti + 1)
            inc(v.tensor_tensor(st(sl, 10), st(sl, 0), st(sl, 1), op=ALU.add))
            f1 = inc(v.tensor_tensor(st(sl, 11), st(sl, 2), st(sl, 3), op=ALU.add))
            wf(f1)
            f2 = inc(v.tensor_tensor(st(sl, 12), st(sl, 10), st(sl, 11), op=ALU.add))
            wf(f2)
            inc(v.tensor_tensor(st(sl, 10), st(sl, 4), st(sl, 5), op=ALU.add))
            f3 = inc(v.tensor_tensor(st(sl, 11), st(sl, 6), st(sl, 7), op=ALU.add))
            wf(f3)
            f4 = inc(v.tensor_tensor(st(sl, 13), st(sl, 10), st(sl, 11), op=ALU.add))
            wf(f4)
            inc(v.tensor_scalar(st(sl, 14), st(sl, 12), 1.0 / D, None, op0=ALU.mult))
            f5 = inc(v.tensor_scalar(st(sl, 15), st(sl, 13), 1.0 / D, EPS,
                                     op0=ALU.mult, op1=ALU.add))
            wf(f5)
            f6 = inc(v.tensor_tensor(st(sl, 10), st(sl, 14), st(sl, 14), op=ALU.mult))
            wf(f6)
            v.tensor_tensor(st(sl, 8), st(sl, 15), st(sl, 10), op=ALU.subtract)\
                .then_inc(S["l_var"], 1)
            v.wait_ge(S["l_sq"], ti + 1)
            f7 = inc(v.reciprocal(st(sl, 10), st(sl, 9)))
            wf(f7)
            for qq in range(4):
                u = 4 * ti + qq
                _w(v, S["l_xt"], 2 * u - 6)
                v.tensor_scalar(
                    xn_tm.ap()[:, (u % 4) * 1024:(u % 4 + 1) * 1024],
                    x_t.ap()[:, (u % 6) * 1024:(u % 6 + 1) * 1024],
                    st(sl, 14), st(sl, 10),
                    op0=ALU.subtract, op1=ALU.mult,
                ).then_inc(S["l_app"], 1)
            for qq in range(4):
                for sub in range(2):
                    gix = 2 * (4 * ti + qq) + sub
                    v.wait_ge(S["l_xt"], gix + 1)
                    kc0 = 8 * qq + 4 * sub
                    v.tensor_copy(
                        xn_fm.ap().rearrange("p (kc t) -> p kc t", t=T)
                            [:, kc0:kc0 + 4, ti * 128:(ti + 1) * 128],
                        XT_ps[gix % 2].ap().rearrange("p (a b) -> p a b", a=4),
                    ).then_inc(S["l_xev"], 1)
            if DO_MASK:
                for _ in range(4):
                    emit_reduce(v, mrq[0])
                if ti == 9:
                    emit_ST(v, 0)
                if ti == 14:
                    emit_ST(v, 1)
        for q in range(mrq[0], NMQ if DO_MASK else 0):
            emit_reduce(v, q)

    @blkpre.tensor
    def _(t):
        t.wait_ge(S["p_ld"], 16 * 6)
        t.wait_ge(S["p_bm"], 16)
        for u in range(NXQ):
            ti, qq = u // 4, u % 4
            for sub in range(2):
                gix = 2 * u + sub
                t.wait_ge(S["l_app"], u + 1)
                _w(t, S["l_xev"], gix - 1)
                for j in range(4):
                    mm = t.transpose(
                        XT_ps[gix % 2].ap()[:, j * 128:(j + 1) * 128],
                        xn_tm.ap()[:, (u % 4) * 1024 + (4 * sub + j) * 128:
                                   (u % 4) * 1024 + (4 * sub + j + 1) * 128],
                        idbf_sb.ap(),
                    )
                mm.then_inc(S["l_xt"], 1)
            if DO_MASK and u == 31:
                emit_strT(t, 0)
            if DO_MASK and u == 51:
                emit_strT(t, 1)
        t.wait_ge(S["l_xev"], 2 * NXQ)
        for gi in range(5 if DO_S1 else 0):
            _w(t, S["g_ev"], 2 * gi)
            for kc in range(KC):
                u = gi * 8 + kc // 4
                if kc % 4 == 0:
                    t.wait_ge(S["g_rdy"], u + 1)
                for th in range(TH):
                    unit = 2 * gi + th
                    for j in range(2):
                        mm = t.matmul(
                            gen_ps[unit % 2].ap()[:, j * 512:(j + 1) * 512],
                            genring.ap()[:, (u % 3) * 512 + (kc % 4) * 128:
                                         (u % 3) * 512 + (kc % 4 + 1) * 128],
                            xn_fm.ap()[:, kc * T + th * THW + j * 512:
                                       kc * T + th * THW + (j + 1) * 512],
                            start=(kc == 0), stop=(kc == KC - 1),
                        )
                if kc % 4 == 3 and kc != KC - 1:
                    mm.then_inc(S["g_use"], 1)
            mm.then_inc(S["g_acc"], 2)

    ln.close()
    qkh = ExitStack()
    genst = qkh.enter_context(nc.sbuf_tensor("genst", [128, 4 * 512], F32))
    q_fm = qkh.enter_context(nc.sbuf_tensor("q_fm", [128, 2 * T], BF16))
    k_fm = qkh.enter_context(nc.sbuf_tensor("k_fm", [128, 2 * T], BF16))
    h_fm = qkh.enter_context(nc.sbuf_tensor("h_fm", [128, T], BF16))
    eselrg = qkh.enter_context(nc.sbuf_tensor("eselrg", [128, 4 * 512], BF16))
    scfm_t = k_fm
    gen_dst = [q_fm.ap()[:, 0:T], q_fm.ap()[:, T:2 * T],
               k_fm.ap()[:, 0:T], k_fm.ap()[:, T:2 * T], h_fm.ap()]
    wfmT = q_fm.ap()[0:8, 0:T]

    @blkpre.sync
    def _(sync):
        if DO_S1:
            sync.wait_ge(S["l_app"], NXQ)
        for u in range(NGS if DO_S1 else 0):
            _w(sync, S["g_rdy"], u - 3)
            gi, s4 = u // 8, u % 8
            sync.dma_start(
                genst.ap()[:, (u % 4) * 512:(u % 4 + 1) * 512],
                _ap(qkh_d, (gi * KC + 4 * s4) * 128 * 128,
                    [[128, 128], [128 * 128, 4], [1, 128]]),
            ).then_inc(S["gst%d" % (u % 4)], 16)

    @blkpre.scalar
    def _(s):
        def cast(u):
            c = u - 3
            if c >= 0:
                if c % 8 == 7:
                    s.wait_ge(S["g_acc"], 2 * (c // 8) + 2)
                else:
                    s.wait_ge(S["g_use"], use_count(c))
            s.wait_ge(S["gst%d" % (u % 4)], 16 * (u // 4 + 1))
            s.activation(
                genring.ap()[:, (u % 3) * 512:(u % 3 + 1) * 512],
                genst.ap()[:, (u % 4) * 512:(u % 4 + 1) * 512],
                AF.Identity, bias=0.0, scale=1.0,
            ).then_inc(S["g_rdy"], 1)

        def evac(gi):
            for th in range(TH):
                unit = 2 * gi + th
                s.wait_ge(S["g_acc"], 2 * gi + 2)
                s.activation(
                    gen_dst[gi][:, th * THW:(th + 1) * THW],
                    gen_ps[unit % 2].ap(),
                    AF.Identity, bias=0.0, scale=1.0,
                ).then_inc(S["g_ev"], 1)

        if DO_S1:
            for u in range(10):
                cast(u)
            for gi in range(5):
                evac(gi)
                for u in range(8 * gi + 10, min(8 * gi + 18, NGS)):
                    cast(u)

    prp.close()
    pr2 = ExitStack()
    sc_ps = pr2.enter_context(nc.psum_tensor("sc_ps", [8, T], F32))
    st_ps = pr2.enter_context(nc.psum_tensor("st_ps", [128, 128], F32))
    wt_ps = pr2.enter_context(nc.psum_tensor("wt_ps", [8, T], BF16))

    @blkpre.tensor
    def _(t):
        if not DO_S1:
            return
        t.wait_ge(S["g_ev"], 10)
        t.wait_ge(S["g_qk"], 2)
        for ch in range(2):
            for j in range(4):
                mm = t.matmul(
                    sc_ps.ap()[:, j * 512:(j + 1) * 512],
                    sel_sb.ap()[:, ch * 8:(ch + 1) * 8],
                    q_fm.ap()[:, ch * T + j * 512:ch * T + (j + 1) * 512],
                    start=(ch == 0), stop=(ch == 1),
                )
        mm.then_inc(S["g_sc"], 1)
        if DO_MASK:
            emit_strT(t, 2)
            emit_strT(t, 3)
        t.wait_ge(S["g_scev"], 1)
        for tg in range(16):
            mm = t.transpose(
                st_ps.ap()[:, tg * 8:(tg + 1) * 8],
                scfm_t.ap().bitcast(F32)[0:8, 0:T][:, tg * 128:(tg + 1) * 128],
                id32_sb.ap()[0:8, 0:8],
            )
        mm.then_inc(S["g_str"], 1)
        if DO_MASK:
            for g8 in range(4, 8):
                emit_strT(t, g8)
        t.wait_ge(S["g_wx"], 1)
        for tg in range(16):
            mm = t.transpose(
                wt_ps.ap()[:, tg * 128:(tg + 1) * 128],
                wexp_tm[:, tg * 8:(tg + 1) * 8],
                idbf_sb.ap(),
            )
        mm.then_inc(S["g_wt"], 1)

    @blkpre.scalar
    def _(s):
        if not DO_S1:
            return
        s.wait_ge(S["g_sc"], 1)
        s.activation(scfm_t.ap().bitcast(F32)[0:8, 0:T], sc_ps.ap(), AF.Identity, bias=0.0, scale=1.0)\
            .then_inc(S["g_scev"], 1)
        s.wait_ge(S["g_sub"], 1)
        s.activation(exp_t, sc_tm, AF.Exp, bias=0.0, scale=ISQ32)\
            .then_inc(S["g_exp"], 1)

    @blkpre.vector
    def _(v):
        def inc(instr):
            fc[0] += 1
            instr.then_inc(S["l_f"], 1)
            return fc[0]

        def wf(n):
            v.wait_ge(S["l_f"], n)

        if not DO_S1:
            return
        v.wait_ge(S["g_ev"], 8)
        for ch in range(2):
            v.tensor_tensor(
                q_fm.ap()[:, ch * T:(ch + 1) * T],
                q_fm.ap()[:, ch * T:(ch + 1) * T],
                k_fm.ap()[:, ch * T:(ch + 1) * T],
                op=ALU.mult,
            ).then_inc(S["g_qk"], 1)
        if DO_MASK:
            emit_ST(v, 2)
            emit_ST(v, 3)
        v.wait_ge(S["g_str"], 1)
        g1 = inc(v.tensor_copy(sc_tm, st_ps.ap()))
        wf(g1)
        g2 = inc(v.tensor_reduce(m_t, sc_tm.rearrange("p (a b) -> p a b", b=8),
                                 axis=mybir.AxisListType.X, op=ALU.max))
        wf(g2)
        v.tensor_tensor(
            sc_tm.rearrange("p (a b) -> p a b", b=8),
            sc_tm.rearrange("p (a b) -> p a b", b=8),
            m_t.broadcast_to((128, 16, 8)), op=ALU.subtract,
        ).then_inc(S["g_sub"], 1)
        if DO_MASK:
            emit_ST(v, 4)
            emit_ST(v, 5)
        v.wait_ge(S["g_exp"], 1)
        g3 = inc(v.tensor_reduce(z_t, exp_t.rearrange("p (a b) -> p a b", b=8),
                                 axis=mybir.AxisListType.X, op=ALU.add))
        wf(g3)
        g4 = inc(v.reciprocal(zr_t, z_t))
        wf(g4)
        if DO_MASK:
            emit_ST(v, 6)
            emit_ST(v, 7)
        v.tensor_tensor(
            wexp_tm.rearrange("p (a b) -> p a b", b=8),
            exp_t.rearrange("p (a b) -> p a b", b=8),
            zr_t.broadcast_to((128, 16, 8)), op=ALU.mult,
        ).then_inc(S["g_wx"], 1)
        v.wait_ge(S["g_wt"], 1)
        v.tensor_copy(wfmT, wt_ps.ap()).then_inc(S["g_wtv"], 1)

    pr2.close()
    pr3 = ExitStack()
    e_ps = pr3.enter_context(nc.psum_tensor("e_ps", [128, THW], F32))
    we_ps = pr3.enter_context(nc.psum_tensor("we_ps", [128, T], F32))

    def emit_e16(t):
        for j in range(4):
            mm = t.matmul(
                we_ps.ap()[:, j * 512:(j + 1) * 512],
                E16_sb.ap(),
                wfmT[:, j * 512:(j + 1) * 512],
                start=True, stop=True,
            )
        mm.then_inc(S["g_we"], 1)

    @blkpre.tensor
    def _(t):
        if DO_S1:
            t.wait_ge(S["g_wtv"], 1)
            emit_e16(t)
        if not DO_MASK:
            return
        t.wait_ge(S["m_bm"], 1)
        t.wait_ge(S["g_ev"], 10)
        if DO_S1:
            t.wait_ge(S["g_wtv"], 1)
        for g8 in range(KC // 8):
            for k in (2 * g8, 2 * g8 + 1):
                t.wait_ge(S["p_e%d" % (k % 4)], 16 * (k // 4 + 1))
            _w(t, S["m_bev"], g8)
            for j in range(8):
                slot = (2 * g8 + j // 4) % 4
                mm = t.matmul(
                    e_ps.ap()[:, j * 128:(j + 1) * 128],
                    eselrg.ap()[:, slot * 512 + (j % 4) * 128:
                                slot * 512 + (j % 4 + 1) * 128],
                    bmT_sb.ap(),
                    start=True, stop=True,
                )
            mm.then_inc(S["m_bexp"], 1)

    @blkpre.vector
    def _(v):
        def inc(instr):
            fc[0] += 1
            instr.then_inc(S["l_f"], 1)
            return fc[0]

        def wf(n):
            v.wait_ge(S["l_f"], n)

        if DO_S1:
            v.wait_ge(S["g_we"], 1)
            v.wait_ge(S["g_ev"], 10)
        for j in range(4 if DO_S1 else 0):
            v.tensor_tensor(
                wh_fm.ap()[:, j * 512:(j + 1) * 512],
                h_fm.ap()[:, j * 512:(j + 1) * 512],
                we_ps.ap()[:, j * 512:(j + 1) * 512],
                op=ALU.mult,
            ).then_inc(S["g_wh"], 1)
        if DO_MASK:
            v.wait_ge(S["m_trv"], 8)
            fz = inc(v.tensor_scalar(bmT_sb.ap(), ST.ap(), 0.0, None, op0=ALU.is_gt))
            wf(fz)
            v.nop().then_inc(S["m_bm"], 1)
        for g8 in range(KC // 8 if DO_MASK else 0):
            v.wait_ge(S["m_bexp"], g8 + 1)
            v.tensor_copy(
                bmE_pT.ap()[:, g8 * 1024:(g8 + 1) * 1024],
                e_ps.ap(),
            ).then_inc(S["m_bev"], 1)
        v.wait_ge(S["p_ld"], 16 * 6)
        v.tensor_tensor(biassum.ap(), biassum.ap(), biasB_sb.ap(), op=ALU.add)

    @blkpre.sync
    def _(sync):
        if DO_MASK:
            sync.wait_ge(S["l_app"], NXQ)
        for k in range(8 if DO_MASK else 0):
            _w(sync, S["m_bexp"], k // 2 - 1)
            sync.dma_start(eselrg.ap()[:, (k % 4) * 512:(k % 4 + 1) * 512],
                           Esel_d[:, k * 512:(k + 1) * 512])\
                .then_inc(S["p_e%d" % (k % 4)], 16)

    blkpre.__exit__(None, None, None)
    pr3.close()
    prpT.close()
    qkh.close()
    pre.close()

    mn = ExitStack()
    wm_ring = mn.enter_context(nc.sbuf_tensor("wm_ring", [128, 2 * D], BF16))
    WT_ring = mn.enter_context(nc.sbuf_tensor("WT_ring", [128, 2 * D], BF16))
    x_res = mn.enter_context(nc.sbuf_tensor("x_res", [128, 4 * THW], F32))
    stuff = mn.enter_context(nc.sbuf_tensor("stuff", [128, 2 * THW], F32))
    mnp = ExitStack()
    ACC0 = mnp.enter_context(nc.psum_tensor("ACC0", [128, THW], F32))
    ACC1 = mnp.enter_context(nc.psum_tensor("ACC1", [128, THW], F32))
    T_ps0 = mnp.enter_context(nc.psum_tensor("T_ps0", [128, THW], F32))
    T_ps1 = mnp.enter_context(nc.psum_tensor("T_ps1", [128, THW], F32))
    ACC = [ACC0, ACC1]
    T_ps = [T_ps0, T_ps1]

    def x_res_ap(th, nc_i):
        return _ap(x_d, (th * THW) * D + nc_i * 128,
                   [[D, 128], [128 * D, 8], [1, 128]])

    def out_ap(th, nc_i, j):
        return _ap(out_d, (th * THW + j * 512) * D + nc_i * 128,
                   [[D, 128], [128 * D, 4], [1, 128]])

    def emit_tr(t, m):
        for th in range(TH):
            t.wait_ge(S["a_stuff"], 2 * m + th + 1)
            _w(t, S["a_add"], (4 * (m - 1) + 2 * th + 2) if m >= 1 else 0)
            for j8 in range(8):
                mm = t.transpose(
                    T_ps[th].ap()[:, j8 * 128:(j8 + 1) * 128],
                    stuff.ap()[:, th * THW + j8 * 128:th * THW + (j8 + 1) * 128],
                    id32_sb.ap(),
                )
            mm.then_inc(S["a_tr"], 1)

    with nc.Block() as blk:
        @blk.gpsimd
        def _(g):
            for nc_i in range(NC):
                _w(g, S["a_acc"], 2 * (nc_i - 1))
                g.dma_start(
                    wm_ring.ap()[:, (nc_i % 2) * D:(nc_i % 2 + 1) * D],
                    _ap(wTb_d, nc_i * 128 * D, [[D, 128], [1, D]]),
                    max_dma_last_dim=8192,
                ).then_inc(S["a_wld%d" % (nc_i % 2)], 16)
                g.dma_start(
                    WT_ring.ap()[:, (nc_i % 2) * D:(nc_i % 2 + 1) * D],
                    _ap(WTb_d, nc_i * 128 * D, [[D, 128], [1, D]]),
                    max_dma_last_dim=8192,
                ).then_inc(S["a_Wld%d" % (nc_i % 2)], 16)

        @blk.vector
        def _(v):
            for i in range(NC + 1):
                if i < NC:
                    v.wait_ge(S["a_wld%d" % (i % 2)], 16 * (i // 2 + 1))
                    w_view = wm_ring.ap()[:, (i % 2) * D:(i % 2 + 1) * D]\
                        .rearrange("p (kc q r) -> p kc q r", kc=KC, q=4)
                    v.tensor_tensor(
                        w_view, w_view,
                        bmE_pT.ap().rearrange("p (kc nb) -> p kc nb", nb=128)
                            [:, :, 4 * i:4 * i + 4].broadcast_to((128, KC, 4, 32)),
                        op=ALU.mult,
                    ).then_inc(S["a_wmask"], 1)
                if i >= 1:
                    nc_i = i - 1
                    for th in range(TH):
                        for j in range(2):
                            xr = x_res.ap()[:, (nc_i % 2) * 2 * THW + th * THW + j * 512:
                                            (nc_i % 2) * 2 * THW + th * THW + (j + 1) * 512]
                            v.wait_ge(S["a_tr"], 2 * nc_i + th + 1)
                            v.wait_ge(S["a_xr%d" % (2 * th + nc_i % 2)],
                                      16 * (nc_i // 2 + 1))
                            v.tensor_tensor(
                                xr,
                                T_ps[th].ap()[:, j * 512:(j + 1) * 512],
                                xr,
                                op=ALU.add,
                            ).then_inc(S["a_add"], 1)

        @blk.tensor
        def _(t):
            for nc_i in range(NC):
                if nc_i >= 1:
                    emit_tr(t, nc_i - 1)
                t.wait_ge(S["a_wmask"], nc_i + 1)
                t.wait_ge(S["a_Wld%d" % (nc_i % 2)], 16 * (nc_i // 2 + 1))
                sl = nc_i % 2
                for th in range(TH):
                    _w(t, S["a_stuff"], 2 * (nc_i - 1) + th + 1)
                    for kc in range(KC):
                        for ring in (wm_ring, WT_ring):
                            for j in range(2):
                                mm = t.matmul(
                                    ACC[th].ap()[:, j * 512:(j + 1) * 512],
                                    ring.ap()[:, sl * D + kc * 128:sl * D + (kc + 1) * 128],
                                    xn_fm.ap()[:, kc * T + th * THW + j * 512:
                                               kc * T + th * THW + (j + 1) * 512],
                                    start=(kc == 0 and ring is wm_ring),
                                    stop=False,
                                )
                    for j in range(2):
                        mm = t.matmul(
                            ACC[th].ap()[:, j * 512:(j + 1) * 512],
                            BmT_sb.ap()[:, nc_i * 128:(nc_i + 1) * 128],
                            wh_fm.ap()[:, th * THW + j * 512:th * THW + (j + 1) * 512],
                            start=False, stop=True,
                        )
                    mm.then_inc(S["a_acc"], 1)
            emit_tr(t, NC - 1)

        @blk.scalar
        def _(s):
            for nc_i in range(NC):
                for th in range(TH):
                    s.wait_ge(S["a_acc"], 2 * nc_i + th + 1)
                    _w(s, S["a_tr"], 2 * (nc_i - 1) + th + 1)
                    s.activation(
                        stuff.ap()[:, th * THW:(th + 1) * THW],
                        ACC[th].ap(),
                        AF.Identity,
                        bias=biassum.ap()[:, nc_i:nc_i + 1],
                        scale=1.0,
                    ).then_inc(S["a_stuff"], 1)

        @blk.sync
        def _(sync):
            for i in range(NC + 1):
                if i < NC:
                    for th in range(TH):
                        if i >= 2:
                            sync.wait_ge(S["a_st%d" % (2 * th)], 16 * (i - 1))
                            sync.wait_ge(S["a_st%d" % (2 * th + 1)], 16 * (i - 1))
                        sync.dma_start(
                            x_res.ap()[:, (i % 2) * 2 * THW + th * THW:
                                       (i % 2) * 2 * THW + (th + 1) * THW],
                            x_res_ap(th, i),
                        ).then_inc(S["a_xr%d" % (2 * th + i % 2)], 16)
                if i >= 1:
                    nc_i = i - 1
                    for th in range(TH):
                        for j in range(2):
                            gidx = 4 * nc_i + 2 * th + j
                            sj = 2 * th + j
                            sync.wait_ge(S["a_add"], gidx + 1)
                            sync.dma_start(
                                out_ap(th, nc_i, j),
                                x_res.ap()[:, (nc_i % 2) * 2 * THW + th * THW + j * 512:
                                           (nc_i % 2) * 2 * THW + th * THW + (j + 1) * 512],
                            ).then_inc(S["a_st%d" % sj], 16)
            for sj in range(4):
                sync.wait_ge(S["a_st%d" % sj], 16 * 32)

    mnp.close()
    mn.close()
    es.close()
    return nc


def _host_constants():
    p = np.arange(128)
    blk = p // 32
    E3 = np.zeros((128, 32, 128), np.float32)
    ET3 = np.zeros((128, 32, 128), np.float32)
    for kc in range(32):
        E3[4 * kc + blk, kc, p] = 1.0
        ET3[p, kc, 4 * kc + blk] = 1.0
    Esel = E3.reshape(128, 4096).astype(ml_dtypes.bfloat16)
    ETsel = ET3.reshape(128, 4096).astype(ml_dtypes.bfloat16)
    sel = np.zeros((128, 16), np.float32)
    for ch in range(2):
        sel[p, ch * 8 + ch * 4 + blk] = 1.0
    selqk = sel.astype(ml_dtypes.bfloat16)
    E16 = np.zeros((8, 128), np.float32)
    E16[np.arange(128) // 16, np.arange(128)] = 1.0
    E16 = E16.astype(ml_dtypes.bfloat16)
    id32 = np.eye(128, dtype=np.float32)
    idbf = np.eye(128).astype(ml_dtypes.bfloat16)
    return Esel, ETsel, selqk, E16, id32, idbf


def kernel(L0_attn_data, L0_attn_mask, L0_attn_weight, L0_attn_bias,
           L0_ffn_W, L0_ffn_b, L0_ffn_Wq, L0_ffn_Wk, L0_ffn_A, L0_ffn_Bm):
    x_full = np.ascontiguousarray(np.asarray(L0_attn_data, np.float32)).reshape(16384, 4096)
    mask = np.ascontiguousarray(np.asarray(L0_attn_mask, np.int32))
    w = np.asarray(L0_attn_weight, np.float32)
    W = np.asarray(L0_ffn_W, np.float32)

    def blocked_T(m):
        return np.ascontiguousarray(
            m.reshape(32, 128, 32, 128).transpose(0, 3, 2, 1).reshape(32, 128, 4096))

    wTb = blocked_T(w)
    WTb = blocked_T(W)

    Wq = np.asarray(L0_ffn_Wq, np.float32)
    Wk = np.asarray(L0_ffn_Wk, np.float32)
    A = np.asarray(L0_ffn_A, np.float32).reshape(128, 4096)
    qkh = np.concatenate([Wq.T, Wk.T, A.T], axis=1)
    qkhTb = np.ascontiguousarray(
        qkh.reshape(32, 128, 5, 128).transpose(2, 0, 1, 3))

    BmT = np.ascontiguousarray(
        np.asarray(L0_ffn_Bm, np.float32).transpose(0, 2, 1).reshape(128, 4096))
    biasA = np.ascontiguousarray(np.asarray(L0_attn_bias, np.float32).reshape(32, 128).T)
    biasB = np.ascontiguousarray(np.asarray(L0_ffn_b, np.float32).reshape(32, 128).T)

    Esel, _ETsel_unused, selqk, E16, id32, idbf = _host_constants()

    nc = build_nc()
    shared = dict(mask=mask, wTb=wTb, WTb=WTb, qkhTb=qkhTb, BmT=BmT,
                  biasA=biasA, biasB=biasB, Esel=Esel,
                  selqk=selqk, E16=E16, id32=id32, idbf=idbf)
    in_maps = [dict(x=np.ascontiguousarray(x_full[c * 2048:(c + 1) * 2048]), **shared)
               for c in range(NCORE)]
    res = run_bass_kernel_spmd(nc, in_maps, core_ids=list(range(NCORE)))
    out = np.concatenate([np.asarray(res.results[c]["out"]) for c in range(NCORE)], axis=0)
    return out.reshape(8, 2048, 4096)


# version 28
# speedup vs baseline: 1.0189x; 1.0175x over previous
import numpy as np
import ml_dtypes
from contextlib import ExitStack

import concourse.bass as bass
import concourse.mybir as mybir
from concourse.bass_utils import run_bass_kernel_spmd

F32 = mybir.dt.float32
BF16 = mybir.dt.bfloat16
I32 = mybir.dt.int32
AF = mybir.ActivationFunctionType
ALU = mybir.AluOpType

NCORE = 8
T = 2048
D = 4096
KC = 32
NC = 32
TH = 2
THW = 1024
EPS = 1e-5
ISQ32 = float(1.0 / np.sqrt(32.0))
NXQ = 64
NMQ = 128
NGS = 40
DO_MASK = True
DO_S1 = True


def _w(eng, sem, v):
    if v > 0:
        eng.wait_ge(sem, v)


def _ap(base, offset, dims):
    return bass.AP(base.tensor, base.offset + offset, dims)


def build_nc():
    nc = bass.Bass("TRN2", target_bir_lowering=False)

    x_d = nc.dram_tensor("x", [T, D], F32, kind="ExternalInput").ap()
    mask_d = nc.dram_tensor("mask", [D, D], I32, kind="ExternalInput").ap()
    wTb_d = nc.dram_tensor("wTb", [NC, 128, D], F32, kind="ExternalInput").ap()
    WTb_d = nc.dram_tensor("WTb", [NC, 128, D], F32, kind="ExternalInput").ap()
    qkh_d = nc.dram_tensor("qkhTb", [5, KC, 128, 128], F32, kind="ExternalInput").ap()
    BmT_d = nc.dram_tensor("BmT", [128, D], F32, kind="ExternalInput").ap()
    biasA_d = nc.dram_tensor("biasA", [128, 32], F32, kind="ExternalInput").ap()
    biasB_d = nc.dram_tensor("biasB", [128, 32], F32, kind="ExternalInput").ap()
    Esel_d = nc.dram_tensor("Esel", [128, D], BF16, kind="ExternalInput").ap()
    sel_d = nc.dram_tensor("selqk", [128, 16], BF16, kind="ExternalInput").ap()
    E16_d = nc.dram_tensor("E16", [8, 128], BF16, kind="ExternalInput").ap()
    id32_d = nc.dram_tensor("id32", [128, 128], F32, kind="ExternalInput").ap()
    idbf_d = nc.dram_tensor("idbf", [128, 128], BF16, kind="ExternalInput").ap()
    out_d = nc.dram_tensor("out", [T, D], F32, kind="ExternalOutput").ap()

    xn_fm = nc.alloc_sbuf_tensor("xn_fm", [128, KC * T], BF16)
    wh_fm = nc.alloc_sbuf_tensor("wh_fm", [128, T], BF16)
    BmT_sb = nc.alloc_sbuf_tensor("BmT_sb", [128, D], BF16)
    bmE_pT = nc.alloc_sbuf_tensor("bmE_pT", [128, D], BF16)
    biassum = nc.alloc_sbuf_tensor("biassum", [128, 32], F32)
    biasB_sb = nc.alloc_sbuf_tensor("biasB_sb", [128, 32], F32)
    sel_sb = nc.alloc_sbuf_tensor("sel_sb", [128, 16], BF16)
    E16_sb = nc.alloc_sbuf_tensor("E16_sb", [8, 128], BF16)
    id32_sb = nc.alloc_sbuf_tensor("id32_sb", [128, 128], F32)
    idbf_sb = nc.alloc_sbuf_tensor("idbf_sb", [128, 128], BF16)
    bmT_sb = nc.alloc_sbuf_tensor("bmT_sb", [128, 128], BF16)
    ST = nc.alloc_sbuf_tensor("ST", [128, 128], F32)

    es = ExitStack()
    S = {}
    for name in [
        "p_ld", "p_bm", "p_e0", "p_e1", "p_e2", "p_e3",
        "l_x0", "l_x1", "l_x2", "l_x3", "l_x4", "l_x5", "l_x6", "l_xq", "l_ssq", "l_var", "l_sq", "l_app",
        "l_xt", "l_xev", "l_f",
        "m_mk0", "m_mk1", "m_mk2", "m_mk3", "m_mk4", "m_mk5", "m_s1", "m_tr", "m_trv",
        "m_bm", "m_bexp", "m_bev",
        "gst0", "gst1", "gst2", "gst3", "gst4", "g_rdy", "g_use", "g_acc", "g_ev",
        "g_qk", "g_sc", "g_scev", "g_str", "g_sub", "g_exp",
        "g_wx", "g_wt", "g_wtv", "g_we", "g_wh",
        "a_wld0", "a_wld1", "a_wld2", "a_wld3", "a_Wld0", "a_Wld1", "a_Wld2", "a_Wld3", "a_wmask", "a_acc", "a_stuff",
        "a_tr", "a_add", "a_xr0", "a_xr1", "a_xr2", "a_xr3", "a_st0", "a_st1", "a_st2", "a_st3",
    ]:
        S[name] = es.enter_context(nc.semaphore(name))

    with nc.Block() as blk:
        @blk.sync
        def _(sync):
            for h in S.values():
                sync.sem_clear(h)

    with nc.Block() as blk:
        @blk.sync
        def _(sync):
            sync.dma_start(biassum.ap(), biasA_d).then_inc(S["p_ld"], 16)
            sync.dma_start(biasB_sb.ap(), biasB_d).then_inc(S["p_ld"], 16)
            sync.dma_start(sel_sb.ap(), sel_d).then_inc(S["p_ld"], 16)
            sync.dma_start(E16_sb.ap(), E16_d).then_inc(S["p_ld"], 16)
            sync.dma_start(id32_sb.ap(), id32_d).then_inc(S["p_ld"], 16)
            sync.dma_start(idbf_sb.ap(), idbf_d).then_inc(S["p_ld"], 16)

        @blk.gpsimd
        def _(g):
            g.dma_start(BmT_sb.ap(), BmT_d).then_inc(S["p_bm"], 16)

    pre = ExitStack()
    S1k = pre.enter_context(nc.sbuf_tensor("S1k", [128, D], BF16))
    mk_st = pre.enter_context(nc.sbuf_tensor("mk_st", [128, 5 * 1024], BF16))
    genring = pre.enter_context(nc.sbuf_tensor("genring", [128, 3 * 512], BF16))
    stats = pre.enter_context(nc.sbuf_tensor("stats", [128, 32], F32))

    gr32 = genring.ap().bitcast(F32)
    sc_tm = gr32[:, 0:128]
    exp_t = gr32[:, 128:256]
    m_t = gr32[:, 256:272]
    z_t = gr32[:, 272:288]
    zr_t = gr32[:, 288:304]
    wexp_tm = genring.ap()[:, 608:736]

    ln = ExitStack()
    x_t = ln.enter_context(nc.sbuf_tensor("x_t", [128, 6 * 1024], F32))
    xsq = ln.enter_context(nc.sbuf_tensor("xsq", [128, 1024], BF16))
    xn_tm = ln.enter_context(nc.sbuf_tensor("xn_tm", [128, 4 * 1024], BF16))

    prpT = ExitStack()
    tr_ps = prpT.enter_context(nc.psum_tensor("tr_ps", [128, 512], BF16))
    prp = ExitStack()
    XT_ps = [prp.enter_context(nc.psum_tensor("XT_ps%d" % i, [128, 512], BF16))
             for i in range(2)]
    gen_ps = [prp.enter_context(nc.psum_tensor("gen_ps%d" % i, [128, THW], F32))
              for i in range(2)]

    def st(sl, col):
        return stats.ap()[:, sl * 16 + col:sl * 16 + col + 1]

    def emit_strT(t, g8):
        t.wait_ge(S["m_s1"], 16 * (g8 + 1))
        _w(t, S["m_trv"], g8)
        for j in range(4):
            mm = t.transpose(
                tr_ps.ap()[:, j * 128:(j + 1) * 128],
                S1k.ap()[:, (4 * g8 + j) * 128:(4 * g8 + j + 1) * 128],
                idbf_sb.ap(),
            )
        mm.then_inc(S["m_tr"], 1)

    def emit_ST(v, g8):
        v.wait_ge(S["m_tr"], g8 + 1)
        v.tensor_reduce(
            ST.ap()[:, g8 * 16:(g8 + 1) * 16],
            tr_ps.ap().rearrange("p (a b) -> p a b", b=32),
            axis=mybir.AxisListType.X, op=ALU.add,
        ).then_inc(S["m_trv"], 1)

    blkpre = nc.Block().__enter__()

    @blkpre.sync
    def _(sync):
        for xq in range(NXQ):
            _w(sync, S["l_app"], xq - 5)
            ti, qq = xq // 4, xq % 4
            sync.dma_start(
                x_t.ap()[:, (xq % 6) * 1024:(xq % 6 + 1) * 1024],
                x_d[ti * 128:(ti + 1) * 128, qq * 1024:(qq + 1) * 1024],
            ).then_inc(S["l_x%d" % (xq % 6)], 16)


    def use_count(s):
        return (s + 1) - (s // 8) - (1 if s % 8 == 7 else 0)

    mq = [0]

    def emit_mask(g, n):
        if not DO_MASK:
            return
        for _ in range(n):
            q = mq[0]
            if q >= NMQ:
                return
            mq[0] += 1
            _w(g, S["m_s1"], q - 4)
            mt, qq = q // 4, q % 4
            g.dma_start(
                mk_st.ap()[:, (q % 5) * 1024:(q % 5 + 1) * 1024],
                mask_d[mt * 128:(mt + 1) * 128, qq * 1024:(qq + 1) * 1024],
                max_dma_last_dim=8192,
            ).then_inc(S["m_mk%d" % (q % 5)], 16)

    @blkpre.gpsimd
    def _(g):
        emit_mask(g, NMQ)

    @blkpre.scalar
    def _(s):
        for ti in range(16):
            sl = ti % 2
            if ti >= 1:
                s.wait_ge(S["l_var"], ti)
                s.activation(st((ti - 1) % 2, 9), st((ti - 1) % 2, 8), AF.Sqrt,
                             bias=0.0, scale=1.0).then_inc(S["l_sq"], 1)
            for qq in range(4):
                xq = 4 * ti + qq
                s.wait_ge(S["l_x%d" % (xq % 6)], 16 * (xq // 6 + 1))
                xa = x_t.ap()[:, (xq % 6) * 1024:(xq % 6 + 1) * 1024]
                _w(s, S["l_xq"], 8 * ti + 2 * qq)
                s.activation(xsq.ap(), xa, AF.Identity, bias=0.0, scale=1.0,
                             accum_out=st(sl, qq)).then_inc(S["l_xq"], 1)
                s.wait_ge(S["l_xq"], 8 * ti + 2 * qq + 1)
                s.activation(xsq.ap(), xa, AF.Square,
                             accum_out=st(sl, 4 + qq)).then_inc(S["l_xq"], 1)
            s.activation(st(sl, 9), st(sl, 0), AF.Identity, bias=0.0,
                         scale=1.0).then_inc(S["l_ssq"], 1)
        s.wait_ge(S["l_var"], 16)
        s.activation(st(15 % 2, 9), st(15 % 2, 8), AF.Sqrt, bias=0.0, scale=1.0)\
            .then_inc(S["l_sq"], 1)

    fc = [0]

    mrq = [0]

    def emit_reduce(v, q):
        v.wait_ge(S["m_mk%d" % (q % 5)], 16 * (q // 5 + 1))
        mt, qq = q // 4, q % 4
        with nc.allow_low_precision(reason="0/1 mask 32-sums <=32 exact in bf16"):
            v.tensor_reduce(
                S1k.ap()[:, mt * 128 + qq * 32:mt * 128 + (qq + 1) * 32],
                mk_st.ap()[:, (q % 5) * 1024:(q % 5 + 1) * 1024]
                    .rearrange("p (a b) -> p a b", b=32),
                axis=mybir.AxisListType.X, op=ALU.add,
            ).then_inc(S["m_s1"], 1)
        mrq[0] += 1

    @blkpre.vector
    def _(v):
        def inc(instr):
            fc[0] += 1
            instr.then_inc(S["l_f"], 1)
            return fc[0]

        def wf(n):
            v.wait_ge(S["l_f"], n)

        for ti in range(16):
            sl = ti % 2
            v.wait_ge(S["l_ssq"], ti + 1)
            inc(v.tensor_tensor(st(sl, 10), st(sl, 0), st(sl, 1), op=ALU.add))
            f1 = inc(v.tensor_tensor(st(sl, 11), st(sl, 2), st(sl, 3), op=ALU.add))
            wf(f1)
            f2 = inc(v.tensor_tensor(st(sl, 12), st(sl, 10), st(sl, 11), op=ALU.add))
            wf(f2)
            inc(v.tensor_tensor(st(sl, 10), st(sl, 4), st(sl, 5), op=ALU.add))
            f3 = inc(v.tensor_tensor(st(sl, 11), st(sl, 6), st(sl, 7), op=ALU.add))
            wf(f3)
            f4 = inc(v.tensor_tensor(st(sl, 13), st(sl, 10), st(sl, 11), op=ALU.add))
            wf(f4)
            inc(v.tensor_scalar(st(sl, 14), st(sl, 12), 1.0 / D, None, op0=ALU.mult))
            f5 = inc(v.tensor_scalar(st(sl, 15), st(sl, 13), 1.0 / D, EPS,
                                     op0=ALU.mult, op1=ALU.add))
            wf(f5)
            f6 = inc(v.tensor_tensor(st(sl, 10), st(sl, 14), st(sl, 14), op=ALU.mult))
            wf(f6)
            v.tensor_tensor(st(sl, 8), st(sl, 15), st(sl, 10), op=ALU.subtract)\
                .then_inc(S["l_var"], 1)
            v.wait_ge(S["l_sq"], ti + 1)
            f7 = inc(v.reciprocal(st(sl, 10), st(sl, 9)))
            wf(f7)
            for qq in range(4):
                u = 4 * ti + qq
                _w(v, S["l_xt"], 2 * u - 6)
                v.tensor_scalar(
                    xn_tm.ap()[:, (u % 4) * 1024:(u % 4 + 1) * 1024],
                    x_t.ap()[:, (u % 6) * 1024:(u % 6 + 1) * 1024],
                    st(sl, 14), st(sl, 10),
                    op0=ALU.subtract, op1=ALU.mult,
                ).then_inc(S["l_app"], 1)
            for qq in range(4):
                for sub in range(2):
                    gix = 2 * (4 * ti + qq) + sub
                    v.wait_ge(S["l_xt"], gix + 1)
                    kc0 = 8 * qq + 4 * sub
                    v.tensor_copy(
                        xn_fm.ap().rearrange("p (kc t) -> p kc t", t=T)
                            [:, kc0:kc0 + 4, ti * 128:(ti + 1) * 128],
                        XT_ps[gix % 2].ap().rearrange("p (a b) -> p a b", a=4),
                    ).then_inc(S["l_xev"], 1)
            if DO_MASK:
                for _ in range(4):
                    emit_reduce(v, mrq[0])
                if ti == 9:
                    emit_ST(v, 0)
                if ti == 14:
                    emit_ST(v, 1)
        for q in range(mrq[0], NMQ if DO_MASK else 0):
            emit_reduce(v, q)

    @blkpre.tensor
    def _(t):
        t.wait_ge(S["p_ld"], 16 * 6)
        t.wait_ge(S["p_bm"], 16)
        for u in range(NXQ):
            ti, qq = u // 4, u % 4
            for sub in range(2):
                gix = 2 * u + sub
                t.wait_ge(S["l_app"], u + 1)
                _w(t, S["l_xev"], gix - 1)
                for j in range(4):
                    mm = t.transpose(
                        XT_ps[gix % 2].ap()[:, j * 128:(j + 1) * 128],
                        xn_tm.ap()[:, (u % 4) * 1024 + (4 * sub + j) * 128:
                                   (u % 4) * 1024 + (4 * sub + j + 1) * 128],
                        idbf_sb.ap(),
                    )
                mm.then_inc(S["l_xt"], 1)
            if DO_MASK and u == 31:
                emit_strT(t, 0)
            if DO_MASK and u == 51:
                emit_strT(t, 1)
        t.wait_ge(S["l_xev"], 2 * NXQ)
        for gi in range(5 if DO_S1 else 0):
            _w(t, S["g_ev"], 2 * gi)
            for kc in range(KC):
                u = gi * 8 + kc // 4
                if kc % 4 == 0:
                    t.wait_ge(S["g_rdy"], u + 1)
                for th in range(TH):
                    unit = 2 * gi + th
                    for j in range(2):
                        mm = t.matmul(
                            gen_ps[unit % 2].ap()[:, j * 512:(j + 1) * 512],
                            genring.ap()[:, (u % 3) * 512 + (kc % 4) * 128:
                                         (u % 3) * 512 + (kc % 4 + 1) * 128],
                            xn_fm.ap()[:, kc * T + th * THW + j * 512:
                                       kc * T + th * THW + (j + 1) * 512],
                            start=(kc == 0), stop=(kc == KC - 1),
                        )
                if kc % 4 == 3 and kc != KC - 1:
                    mm.then_inc(S["g_use"], 1)
            mm.then_inc(S["g_acc"], 2)

    ln.close()
    qkh = ExitStack()
    genst = qkh.enter_context(nc.sbuf_tensor("genst", [128, 4 * 512], F32))
    q_fm = qkh.enter_context(nc.sbuf_tensor("q_fm", [128, 2 * T], BF16))
    k_fm = qkh.enter_context(nc.sbuf_tensor("k_fm", [128, 2 * T], BF16))
    h_fm = qkh.enter_context(nc.sbuf_tensor("h_fm", [128, T], BF16))
    eselrg = qkh.enter_context(nc.sbuf_tensor("eselrg", [128, 4 * 512], BF16))
    scfm_t = k_fm
    gen_dst = [q_fm.ap()[:, 0:T], q_fm.ap()[:, T:2 * T],
               k_fm.ap()[:, 0:T], k_fm.ap()[:, T:2 * T], h_fm.ap()]
    wfmT = q_fm.ap()[0:8, 0:T]

    @blkpre.sync
    def _(sync):
        if DO_S1:
            sync.wait_ge(S["l_app"], NXQ)
        for u in range(NGS if DO_S1 else 0):
            _w(sync, S["g_rdy"], u - 3)
            gi, s4 = u // 8, u % 8
            sync.dma_start(
                genst.ap()[:, (u % 4) * 512:(u % 4 + 1) * 512],
                _ap(qkh_d, (gi * KC + 4 * s4) * 128 * 128,
                    [[128, 128], [128 * 128, 4], [1, 128]]),
            ).then_inc(S["gst%d" % (u % 4)], 16)

    @blkpre.scalar
    def _(s):
        def cast(u):
            c = u - 3
            if c >= 0:
                if c % 8 == 7:
                    s.wait_ge(S["g_acc"], 2 * (c // 8) + 2)
                else:
                    s.wait_ge(S["g_use"], use_count(c))
            s.wait_ge(S["gst%d" % (u % 4)], 16 * (u // 4 + 1))
            s.activation(
                genring.ap()[:, (u % 3) * 512:(u % 3 + 1) * 512],
                genst.ap()[:, (u % 4) * 512:(u % 4 + 1) * 512],
                AF.Identity, bias=0.0, scale=1.0,
            ).then_inc(S["g_rdy"], 1)

        def evac(gi):
            for th in range(TH):
                unit = 2 * gi + th
                s.wait_ge(S["g_acc"], 2 * gi + 2)
                s.activation(
                    gen_dst[gi][:, th * THW:(th + 1) * THW],
                    gen_ps[unit % 2].ap(),
                    AF.Identity, bias=0.0, scale=1.0,
                ).then_inc(S["g_ev"], 1)

        if DO_S1:
            for u in range(10):
                cast(u)
            for gi in range(5):
                evac(gi)
                for u in range(8 * gi + 10, min(8 * gi + 18, NGS)):
                    cast(u)

    prp.close()
    pr2 = ExitStack()
    sc_ps = pr2.enter_context(nc.psum_tensor("sc_ps", [8, T], F32))
    st_ps = pr2.enter_context(nc.psum_tensor("st_ps", [128, 128], F32))
    wt_ps = pr2.enter_context(nc.psum_tensor("wt_ps", [8, T], BF16))

    @blkpre.tensor
    def _(t):
        if not DO_S1:
            return
        t.wait_ge(S["g_ev"], 10)
        t.wait_ge(S["g_qk"], 2)
        for ch in range(2):
            for j in range(4):
                mm = t.matmul(
                    sc_ps.ap()[:, j * 512:(j + 1) * 512],
                    sel_sb.ap()[:, ch * 8:(ch + 1) * 8],
                    q_fm.ap()[:, ch * T + j * 512:ch * T + (j + 1) * 512],
                    start=(ch == 0), stop=(ch == 1),
                )
        mm.then_inc(S["g_sc"], 1)
        if DO_MASK:
            emit_strT(t, 2)
            emit_strT(t, 3)
        t.wait_ge(S["g_scev"], 1)
        for tg in range(16):
            mm = t.transpose(
                st_ps.ap()[:, tg * 8:(tg + 1) * 8],
                scfm_t.ap().bitcast(F32)[0:8, 0:T][:, tg * 128:(tg + 1) * 128],
                id32_sb.ap()[0:8, 0:8],
            )
        mm.then_inc(S["g_str"], 1)
        if DO_MASK:
            for g8 in range(4, 8):
                emit_strT(t, g8)
        t.wait_ge(S["g_wx"], 1)
        for tg in range(16):
            mm = t.transpose(
                wt_ps.ap()[:, tg * 128:(tg + 1) * 128],
                wexp_tm[:, tg * 8:(tg + 1) * 8],
                idbf_sb.ap(),
            )
        mm.then_inc(S["g_wt"], 1)

    @blkpre.scalar
    def _(s):
        if not DO_S1:
            return
        s.wait_ge(S["g_sc"], 1)
        s.activation(scfm_t.ap().bitcast(F32)[0:8, 0:T], sc_ps.ap(), AF.Identity, bias=0.0, scale=1.0)\
            .then_inc(S["g_scev"], 1)
        s.wait_ge(S["g_sub"], 1)
        s.activation(exp_t, sc_tm, AF.Exp, bias=0.0, scale=ISQ32)\
            .then_inc(S["g_exp"], 1)

    @blkpre.vector
    def _(v):
        def inc(instr):
            fc[0] += 1
            instr.then_inc(S["l_f"], 1)
            return fc[0]

        def wf(n):
            v.wait_ge(S["l_f"], n)

        if not DO_S1:
            return
        v.wait_ge(S["g_ev"], 8)
        for ch in range(2):
            v.tensor_tensor(
                q_fm.ap()[:, ch * T:(ch + 1) * T],
                q_fm.ap()[:, ch * T:(ch + 1) * T],
                k_fm.ap()[:, ch * T:(ch + 1) * T],
                op=ALU.mult,
            ).then_inc(S["g_qk"], 1)
        if DO_MASK:
            emit_ST(v, 2)
            emit_ST(v, 3)
        v.wait_ge(S["g_str"], 1)
        g1 = inc(v.tensor_copy(sc_tm, st_ps.ap()))
        wf(g1)
        g2 = inc(v.tensor_reduce(m_t, sc_tm.rearrange("p (a b) -> p a b", b=8),
                                 axis=mybir.AxisListType.X, op=ALU.max))
        wf(g2)
        v.tensor_tensor(
            sc_tm.rearrange("p (a b) -> p a b", b=8),
            sc_tm.rearrange("p (a b) -> p a b", b=8),
            m_t.broadcast_to((128, 16, 8)), op=ALU.subtract,
        ).then_inc(S["g_sub"], 1)
        if DO_MASK:
            emit_ST(v, 4)
            emit_ST(v, 5)
        v.wait_ge(S["g_exp"], 1)
        g3 = inc(v.tensor_reduce(z_t, exp_t.rearrange("p (a b) -> p a b", b=8),
                                 axis=mybir.AxisListType.X, op=ALU.add))
        wf(g3)
        g4 = inc(v.reciprocal(zr_t, z_t))
        wf(g4)
        if DO_MASK:
            emit_ST(v, 6)
            emit_ST(v, 7)
        v.tensor_tensor(
            wexp_tm.rearrange("p (a b) -> p a b", b=8),
            exp_t.rearrange("p (a b) -> p a b", b=8),
            zr_t.broadcast_to((128, 16, 8)), op=ALU.mult,
        ).then_inc(S["g_wx"], 1)
        v.wait_ge(S["g_wt"], 1)
        v.tensor_copy(wfmT, wt_ps.ap()).then_inc(S["g_wtv"], 1)

    pr2.close()
    pr3 = ExitStack()
    e_ps = pr3.enter_context(nc.psum_tensor("e_ps", [128, THW], F32))
    we_ps = pr3.enter_context(nc.psum_tensor("we_ps", [128, T], F32))

    def emit_e16(t):
        for j in range(4):
            mm = t.matmul(
                we_ps.ap()[:, j * 512:(j + 1) * 512],
                E16_sb.ap(),
                wfmT[:, j * 512:(j + 1) * 512],
                start=True, stop=True,
            )
        mm.then_inc(S["g_we"], 1)

    @blkpre.tensor
    def _(t):
        if DO_S1:
            t.wait_ge(S["g_wtv"], 1)
            emit_e16(t)
        if not DO_MASK:
            return
        t.wait_ge(S["m_bm"], 1)
        t.wait_ge(S["g_ev"], 10)
        if DO_S1:
            t.wait_ge(S["g_wtv"], 1)
        for g8 in range(KC // 8):
            for k in (2 * g8, 2 * g8 + 1):
                t.wait_ge(S["p_e%d" % (k % 4)], 16 * (k // 4 + 1))
            _w(t, S["m_bev"], g8)
            for j in range(8):
                slot = (2 * g8 + j // 4) % 4
                mm = t.matmul(
                    e_ps.ap()[:, j * 128:(j + 1) * 128],
                    eselrg.ap()[:, slot * 512 + (j % 4) * 128:
                                slot * 512 + (j % 4 + 1) * 128],
                    bmT_sb.ap(),
                    start=True, stop=True,
                )
            mm.then_inc(S["m_bexp"], 1)

    @blkpre.vector
    def _(v):
        def inc(instr):
            fc[0] += 1
            instr.then_inc(S["l_f"], 1)
            return fc[0]

        def wf(n):
            v.wait_ge(S["l_f"], n)

        if DO_S1:
            v.wait_ge(S["g_we"], 1)
            v.wait_ge(S["g_ev"], 10)
        for j in range(4 if DO_S1 else 0):
            v.tensor_tensor(
                wh_fm.ap()[:, j * 512:(j + 1) * 512],
                h_fm.ap()[:, j * 512:(j + 1) * 512],
                we_ps.ap()[:, j * 512:(j + 1) * 512],
                op=ALU.mult,
            ).then_inc(S["g_wh"], 1)
        if DO_MASK:
            v.wait_ge(S["m_trv"], 8)
            fz = inc(v.tensor_scalar(bmT_sb.ap(), ST.ap(), 0.0, None, op0=ALU.is_gt))
            wf(fz)
            v.nop().then_inc(S["m_bm"], 1)
        for g8 in range(KC // 8 if DO_MASK else 0):
            v.wait_ge(S["m_bexp"], g8 + 1)
            v.tensor_copy(
                bmE_pT.ap()[:, g8 * 1024:(g8 + 1) * 1024],
                e_ps.ap(),
            ).then_inc(S["m_bev"], 1)
        v.wait_ge(S["p_ld"], 16 * 6)
        v.tensor_tensor(biassum.ap(), biassum.ap(), biasB_sb.ap(), op=ALU.add)

    @blkpre.sync
    def _(sync):
        if DO_MASK:
            sync.wait_ge(S["l_app"], NXQ)
        for k in range(8 if DO_MASK else 0):
            _w(sync, S["m_bexp"], k // 2 - 1)
            sync.dma_start(eselrg.ap()[:, (k % 4) * 512:(k % 4 + 1) * 512],
                           Esel_d[:, k * 512:(k + 1) * 512])\
                .then_inc(S["p_e%d" % (k % 4)], 16)

    blkpre.__exit__(None, None, None)
    pr3.close()
    prpT.close()
    qkh.close()
    pre.close()

    mn = ExitStack()
    wm_ring = mn.enter_context(nc.sbuf_tensor("wm_ring", [128, 2 * D], BF16))
    WT_ring = mn.enter_context(nc.sbuf_tensor("WT_ring", [128, 2 * D], BF16))
    x_res = mn.enter_context(nc.sbuf_tensor("x_res", [128, 4 * THW], F32))
    stuff = mn.enter_context(nc.sbuf_tensor("stuff", [128, 2 * THW], F32))
    mnp = ExitStack()
    ACC0 = mnp.enter_context(nc.psum_tensor("ACC0", [128, THW], F32))
    ACC1 = mnp.enter_context(nc.psum_tensor("ACC1", [128, THW], F32))
    T_ps0 = mnp.enter_context(nc.psum_tensor("T_ps0", [128, THW], F32))
    T_ps1 = mnp.enter_context(nc.psum_tensor("T_ps1", [128, THW], F32))
    ACC = [ACC0, ACC1]
    T_ps = [T_ps0, T_ps1]

    def x_res_ap(th, nc_i):
        return _ap(x_d, (th * THW) * D + nc_i * 128,
                   [[D, 128], [128 * D, 8], [1, 128]])

    def out_ap(th, nc_i, j):
        return _ap(out_d, (th * THW + j * 512) * D + nc_i * 128,
                   [[D, 128], [128 * D, 4], [1, 128]])

    def emit_tr(t, m):
        for th in range(TH):
            t.wait_ge(S["a_stuff"], 2 * m + th + 1)
            _w(t, S["a_add"], (4 * (m - 1) + 2 * th + 2) if m >= 1 else 0)
            for j8 in range(8):
                mm = t.transpose(
                    T_ps[th].ap()[:, j8 * 128:(j8 + 1) * 128],
                    stuff.ap()[:, th * THW + j8 * 128:th * THW + (j8 + 1) * 128],
                    id32_sb.ap(),
                )
            mm.then_inc(S["a_tr"], 1)

    with nc.Block() as blk:
        @blk.gpsimd
        def _(g):
            for nc_i in range(NC):
                _w(g, S["a_acc"], 2 * (nc_i - 1))
                for h in range(2):
                    g.dma_start(
                        wm_ring.ap()[:, (nc_i % 2) * D + h * 2048:
                                     (nc_i % 2) * D + (h + 1) * 2048],
                        _ap(wTb_d, nc_i * 128 * D + h * 2048, [[D, 128], [1, 2048]]),
                        max_dma_last_dim=8192,
                    ).then_inc(S["a_wld%d" % (2 * (nc_i % 2) + h)], 16)
                for h in range(2):
                    g.dma_start(
                        WT_ring.ap()[:, (nc_i % 2) * D + h * 2048:
                                     (nc_i % 2) * D + (h + 1) * 2048],
                        _ap(WTb_d, nc_i * 128 * D + h * 2048, [[D, 128], [1, 2048]]),
                        max_dma_last_dim=8192,
                    ).then_inc(S["a_Wld%d" % (2 * (nc_i % 2) + h)], 16)

        @blk.vector
        def _(v):
            for i in range(NC + 1):
                if i < NC:
                    for h in range(2):
                        v.wait_ge(S["a_wld%d" % (2 * (i % 2) + h)], 16 * (i // 2 + 1))
                        w_view = wm_ring.ap()[:, (i % 2) * D + h * 2048:
                                              (i % 2) * D + (h + 1) * 2048]\
                            .rearrange("p (kc q r) -> p kc q r", kc=KC // 2, q=4)
                        v.tensor_tensor(
                            w_view, w_view,
                            bmE_pT.ap().rearrange("p (kc nb) -> p kc nb", nb=128)
                                [:, 16 * h:16 * (h + 1), 4 * i:4 * i + 4]
                                .broadcast_to((128, KC // 2, 4, 32)),
                            op=ALU.mult,
                        ).then_inc(S["a_wmask"], 1)
                if i >= 1:
                    nc_i = i - 1
                    for th in range(TH):
                        for j in range(2):
                            xr = x_res.ap()[:, (nc_i % 2) * 2 * THW + th * THW + j * 512:
                                            (nc_i % 2) * 2 * THW + th * THW + (j + 1) * 512]
                            v.wait_ge(S["a_tr"], 2 * nc_i + th + 1)
                            v.wait_ge(S["a_xr%d" % (2 * th + nc_i % 2)],
                                      16 * (nc_i // 2 + 1))
                            v.tensor_tensor(
                                xr,
                                T_ps[th].ap()[:, j * 512:(j + 1) * 512],
                                xr,
                                op=ALU.add,
                            ).then_inc(S["a_add"], 1)

        @blk.tensor
        def _(t):
            for nc_i in range(NC):
                if nc_i >= 1:
                    emit_tr(t, nc_i - 1)
                t.wait_ge(S["a_wmask"], 2 * nc_i + 1)
                t.wait_ge(S["a_Wld%d" % (2 * (nc_i % 2))], 16 * (nc_i // 2 + 1))
                sl = nc_i % 2
                for th in range(TH):
                    _w(t, S["a_stuff"], 2 * (nc_i - 1) + th + 1)
                    for kc in range(KC):
                        if th == 0 and kc == 16:
                            t.wait_ge(S["a_wmask"], 2 * nc_i + 2)
                            t.wait_ge(S["a_Wld%d" % (2 * (nc_i % 2) + 1)],
                                      16 * (nc_i // 2 + 1))
                        for ring in (wm_ring, WT_ring):
                            for j in range(2):
                                mm = t.matmul(
                                    ACC[th].ap()[:, j * 512:(j + 1) * 512],
                                    ring.ap()[:, sl * D + kc * 128:sl * D + (kc + 1) * 128],
                                    xn_fm.ap()[:, kc * T + th * THW + j * 512:
                                               kc * T + th * THW + (j + 1) * 512],
                                    start=(kc == 0 and ring is wm_ring),
                                    stop=False,
                                )
                    for j in range(2):
                        mm = t.matmul(
                            ACC[th].ap()[:, j * 512:(j + 1) * 512],
                            BmT_sb.ap()[:, nc_i * 128:(nc_i + 1) * 128],
                            wh_fm.ap()[:, th * THW + j * 512:th * THW + (j + 1) * 512],
                            start=False, stop=True,
                        )
                    mm.then_inc(S["a_acc"], 1)
            emit_tr(t, NC - 1)

        @blk.scalar
        def _(s):
            for nc_i in range(NC):
                for th in range(TH):
                    s.wait_ge(S["a_acc"], 2 * nc_i + th + 1)
                    _w(s, S["a_tr"], 2 * (nc_i - 1) + th + 1)
                    s.activation(
                        stuff.ap()[:, th * THW:(th + 1) * THW],
                        ACC[th].ap(),
                        AF.Identity,
                        bias=biassum.ap()[:, nc_i:nc_i + 1],
                        scale=1.0,
                    ).then_inc(S["a_stuff"], 1)

        @blk.sync
        def _(sync):
            for i in range(NC + 1):
                if i < NC:
                    for th in range(TH):
                        if i >= 2:
                            sync.wait_ge(S["a_st%d" % (2 * th)], 16 * (i - 1))
                            sync.wait_ge(S["a_st%d" % (2 * th + 1)], 16 * (i - 1))
                        sync.dma_start(
                            x_res.ap()[:, (i % 2) * 2 * THW + th * THW:
                                       (i % 2) * 2 * THW + (th + 1) * THW],
                            x_res_ap(th, i),
                        ).then_inc(S["a_xr%d" % (2 * th + i % 2)], 16)
                if i >= 1:
                    nc_i = i - 1
                    for th in range(TH):
                        for j in range(2):
                            gidx = 4 * nc_i + 2 * th + j
                            sj = 2 * th + j
                            sync.wait_ge(S["a_add"], gidx + 1)
                            sync.dma_start(
                                out_ap(th, nc_i, j),
                                x_res.ap()[:, (nc_i % 2) * 2 * THW + th * THW + j * 512:
                                           (nc_i % 2) * 2 * THW + th * THW + (j + 1) * 512],
                            ).then_inc(S["a_st%d" % sj], 16)
            for sj in range(4):
                sync.wait_ge(S["a_st%d" % sj], 16 * 32)

    mnp.close()
    mn.close()
    es.close()
    return nc


def _host_constants():
    p = np.arange(128)
    blk = p // 32
    E3 = np.zeros((128, 32, 128), np.float32)
    ET3 = np.zeros((128, 32, 128), np.float32)
    for kc in range(32):
        E3[4 * kc + blk, kc, p] = 1.0
        ET3[p, kc, 4 * kc + blk] = 1.0
    Esel = E3.reshape(128, 4096).astype(ml_dtypes.bfloat16)
    ETsel = ET3.reshape(128, 4096).astype(ml_dtypes.bfloat16)
    sel = np.zeros((128, 16), np.float32)
    for ch in range(2):
        sel[p, ch * 8 + ch * 4 + blk] = 1.0
    selqk = sel.astype(ml_dtypes.bfloat16)
    E16 = np.zeros((8, 128), np.float32)
    E16[np.arange(128) // 16, np.arange(128)] = 1.0
    E16 = E16.astype(ml_dtypes.bfloat16)
    id32 = np.eye(128, dtype=np.float32)
    idbf = np.eye(128).astype(ml_dtypes.bfloat16)
    return Esel, ETsel, selqk, E16, id32, idbf


def kernel(L0_attn_data, L0_attn_mask, L0_attn_weight, L0_attn_bias,
           L0_ffn_W, L0_ffn_b, L0_ffn_Wq, L0_ffn_Wk, L0_ffn_A, L0_ffn_Bm):
    x_full = np.ascontiguousarray(np.asarray(L0_attn_data, np.float32)).reshape(16384, 4096)
    mask = np.ascontiguousarray(np.asarray(L0_attn_mask, np.int32))
    w = np.asarray(L0_attn_weight, np.float32)
    W = np.asarray(L0_ffn_W, np.float32)

    def blocked_T(m):
        return np.ascontiguousarray(
            m.reshape(32, 128, 32, 128).transpose(0, 3, 2, 1).reshape(32, 128, 4096))

    wTb = blocked_T(w)
    WTb = blocked_T(W)

    Wq = np.asarray(L0_ffn_Wq, np.float32)
    Wk = np.asarray(L0_ffn_Wk, np.float32)
    A = np.asarray(L0_ffn_A, np.float32).reshape(128, 4096)
    qkh = np.concatenate([Wq.T, Wk.T, A.T], axis=1)
    qkhTb = np.ascontiguousarray(
        qkh.reshape(32, 128, 5, 128).transpose(2, 0, 1, 3))

    BmT = np.ascontiguousarray(
        np.asarray(L0_ffn_Bm, np.float32).transpose(0, 2, 1).reshape(128, 4096))
    biasA = np.ascontiguousarray(np.asarray(L0_attn_bias, np.float32).reshape(32, 128).T)
    biasB = np.ascontiguousarray(np.asarray(L0_ffn_b, np.float32).reshape(32, 128).T)

    Esel, _ETsel_unused, selqk, E16, id32, idbf = _host_constants()

    nc = build_nc()
    shared = dict(mask=mask, wTb=wTb, WTb=WTb, qkhTb=qkhTb, BmT=BmT,
                  biasA=biasA, biasB=biasB, Esel=Esel,
                  selqk=selqk, E16=E16, id32=id32, idbf=idbf)
    in_maps = [dict(x=np.ascontiguousarray(x_full[c * 2048:(c + 1) * 2048]), **shared)
               for c in range(NCORE)]
    res = run_bass_kernel_spmd(nc, in_maps, core_ids=list(range(NCORE)))
    out = np.concatenate([np.asarray(res.results[c]["out"]) for c in range(NCORE)], axis=0)
    return out.reshape(8, 2048, 4096)


# version 29
# speedup vs baseline: 1.0219x; 1.0029x over previous
import numpy as np
import ml_dtypes
from contextlib import ExitStack

import concourse.bass as bass
import concourse.mybir as mybir
from concourse.bass_utils import run_bass_kernel_spmd

F32 = mybir.dt.float32
BF16 = mybir.dt.bfloat16
I32 = mybir.dt.int32
AF = mybir.ActivationFunctionType
ALU = mybir.AluOpType

NCORE = 8
T = 2048
D = 4096
KC = 32
NC = 32
TH = 2
THW = 1024
EPS = 1e-5
ISQ32 = float(1.0 / np.sqrt(32.0))
NXQ = 64
NMQ = 128
NGS = 40
DO_MASK = True
DO_S1 = True


def _w(eng, sem, v):
    if v > 0:
        eng.wait_ge(sem, v)


def _ap(base, offset, dims):
    return bass.AP(base.tensor, base.offset + offset, dims)


def build_nc():
    nc = bass.Bass("TRN2", target_bir_lowering=False)

    x_d = nc.dram_tensor("x", [T, D], F32, kind="ExternalInput").ap()
    mask_d = nc.dram_tensor("mask", [D, D], I32, kind="ExternalInput").ap()
    wTb_d = nc.dram_tensor("wTb", [NC, 128, D], F32, kind="ExternalInput").ap()
    WTb_d = nc.dram_tensor("WTb", [NC, 128, D], F32, kind="ExternalInput").ap()
    qkh_d = nc.dram_tensor("qkhTb", [5, KC, 128, 128], F32, kind="ExternalInput").ap()
    BmT_d = nc.dram_tensor("BmT", [128, D], F32, kind="ExternalInput").ap()
    biasA_d = nc.dram_tensor("biasA", [128, 32], F32, kind="ExternalInput").ap()
    biasB_d = nc.dram_tensor("biasB", [128, 32], F32, kind="ExternalInput").ap()
    Esel_d = nc.dram_tensor("Esel", [128, D], BF16, kind="ExternalInput").ap()
    sel_d = nc.dram_tensor("selqk", [128, 16], BF16, kind="ExternalInput").ap()
    E16_d = nc.dram_tensor("E16", [8, 128], BF16, kind="ExternalInput").ap()
    id32_d = nc.dram_tensor("id32", [128, 128], F32, kind="ExternalInput").ap()
    idbf_d = nc.dram_tensor("idbf", [128, 128], BF16, kind="ExternalInput").ap()
    out_d = nc.dram_tensor("out", [T, D], F32, kind="ExternalOutput").ap()

    xn_fm = nc.alloc_sbuf_tensor("xn_fm", [128, KC * T], BF16)
    wh_fm = nc.alloc_sbuf_tensor("wh_fm", [128, T], BF16)
    BmT_sb = nc.alloc_sbuf_tensor("BmT_sb", [128, D], BF16)
    bmE_pT = nc.alloc_sbuf_tensor("bmE_pT", [128, D], BF16)
    biassum = nc.alloc_sbuf_tensor("biassum", [128, 32], F32)
    biasB_sb = nc.alloc_sbuf_tensor("biasB_sb", [128, 32], F32)
    sel_sb = nc.alloc_sbuf_tensor("sel_sb", [128, 16], BF16)
    E16_sb = nc.alloc_sbuf_tensor("E16_sb", [8, 128], BF16)
    id32_sb = nc.alloc_sbuf_tensor("id32_sb", [128, 128], F32)
    idbf_sb = nc.alloc_sbuf_tensor("idbf_sb", [128, 128], BF16)
    bmT_sb = nc.alloc_sbuf_tensor("bmT_sb", [128, 128], BF16)
    ST = nc.alloc_sbuf_tensor("ST", [128, 128], F32)

    es = ExitStack()
    S = {}
    for name in [
        "p_ld", "p_bm", "p_e0", "p_e1", "p_e2", "p_e3",
        "l_x0", "l_x1", "l_x2", "l_x3", "l_x4", "l_x5", "l_x6", "l_xq", "l_ssq", "l_var", "l_sq", "l_app",
        "l_xt", "l_xev", "l_f",
        "m_mk0", "m_mk1", "m_mk2", "m_mk3", "m_mk4", "m_mk5", "m_s1", "m_tr", "m_trv",
        "m_bm", "m_bexp", "m_bev",
        "gst0", "gst1", "gst2", "gst3", "gst4", "g_rdy", "g_use", "g_acc", "g_ev",
        "g_qk", "g_sc", "g_scev", "g_str", "g_sub", "g_exp",
        "g_wx", "g_wt", "g_wtv", "g_we", "g_wh",
        "a_wld0", "a_wld1", "a_Wld0", "a_Wld1", "a_wmask", "a_acc", "a_stuff",
        "a_tr", "a_add", "a_xr0", "a_xr1", "a_xr2", "a_xr3", "a_st0", "a_st1", "a_st2", "a_st3",
    ]:
        S[name] = es.enter_context(nc.semaphore(name))

    with nc.Block() as blk:
        @blk.sync
        def _(sync):
            for h in S.values():
                sync.sem_clear(h)

    with nc.Block() as blk:
        @blk.sync
        def _(sync):
            sync.dma_start(biassum.ap(), biasA_d).then_inc(S["p_ld"], 16)
            sync.dma_start(biasB_sb.ap(), biasB_d).then_inc(S["p_ld"], 16)
            sync.dma_start(sel_sb.ap(), sel_d).then_inc(S["p_ld"], 16)
            sync.dma_start(E16_sb.ap(), E16_d).then_inc(S["p_ld"], 16)
            sync.dma_start(id32_sb.ap(), id32_d).then_inc(S["p_ld"], 16)
            sync.dma_start(idbf_sb.ap(), idbf_d).then_inc(S["p_ld"], 16)

        @blk.gpsimd
        def _(g):
            g.dma_start(BmT_sb.ap(), BmT_d).then_inc(S["p_bm"], 16)

    pre = ExitStack()
    S1k = pre.enter_context(nc.sbuf_tensor("S1k", [128, D], BF16))
    mk_st = pre.enter_context(nc.sbuf_tensor("mk_st", [128, 5 * 1024], BF16))
    genring = pre.enter_context(nc.sbuf_tensor("genring", [128, 3 * 512], BF16))
    stats = pre.enter_context(nc.sbuf_tensor("stats", [128, 32], F32))

    gr32 = genring.ap().bitcast(F32)
    sc_tm = gr32[:, 0:128]
    exp_t = gr32[:, 128:256]
    m_t = gr32[:, 256:272]
    z_t = gr32[:, 272:288]
    zr_t = gr32[:, 288:304]
    wexp_tm = genring.ap()[:, 608:736]

    ln = ExitStack()
    x_t = ln.enter_context(nc.sbuf_tensor("x_t", [128, 6 * 1024], F32))
    xsq = ln.enter_context(nc.sbuf_tensor("xsq", [128, 1024], BF16))
    xn_tm = ln.enter_context(nc.sbuf_tensor("xn_tm", [128, 4 * 1024], BF16))

    prpT = ExitStack()
    tr_ps = prpT.enter_context(nc.psum_tensor("tr_ps", [128, 512], BF16))
    prp = ExitStack()
    XT_ps = [prp.enter_context(nc.psum_tensor("XT_ps%d" % i, [128, 512], BF16))
             for i in range(2)]
    gen_ps = [prp.enter_context(nc.psum_tensor("gen_ps%d" % i, [128, THW], F32))
              for i in range(2)]

    def st(sl, col):
        return stats.ap()[:, sl * 16 + col:sl * 16 + col + 1]

    def emit_strT(t, g8):
        t.wait_ge(S["m_s1"], 16 * (g8 + 1))
        _w(t, S["m_trv"], g8)
        for j in range(4):
            mm = t.transpose(
                tr_ps.ap()[:, j * 128:(j + 1) * 128],
                S1k.ap()[:, (4 * g8 + j) * 128:(4 * g8 + j + 1) * 128],
                idbf_sb.ap(),
            )
        mm.then_inc(S["m_tr"], 1)

    def emit_ST(v, g8):
        v.wait_ge(S["m_tr"], g8 + 1)
        v.tensor_reduce(
            ST.ap()[:, g8 * 16:(g8 + 1) * 16],
            tr_ps.ap().rearrange("p (a b) -> p a b", b=32),
            axis=mybir.AxisListType.X, op=ALU.add,
        ).then_inc(S["m_trv"], 1)

    blkpre = nc.Block().__enter__()

    @blkpre.sync
    def _(sync):
        for xq in range(NXQ):
            _w(sync, S["l_app"], xq - 5)
            ti, qq = xq // 4, xq % 4
            sync.dma_start(
                x_t.ap()[:, (xq % 6) * 1024:(xq % 6 + 1) * 1024],
                x_d[ti * 128:(ti + 1) * 128, qq * 1024:(qq + 1) * 1024],
            ).then_inc(S["l_x%d" % (xq % 6)], 16)


    def use_count(s):
        return (s + 1) - (s // 8) - (1 if s % 8 == 7 else 0)

    mq = [0]

    def emit_mask(g, n):
        if not DO_MASK:
            return
        for _ in range(n):
            q = mq[0]
            if q >= NMQ:
                return
            mq[0] += 1
            _w(g, S["m_s1"], q - 4)
            mt, qq = q // 4, q % 4
            g.dma_start(
                mk_st.ap()[:, (q % 5) * 1024:(q % 5 + 1) * 1024],
                mask_d[mt * 128:(mt + 1) * 128, qq * 1024:(qq + 1) * 1024],
                max_dma_last_dim=8192,
            ).then_inc(S["m_mk%d" % (q % 5)], 16)

    @blkpre.gpsimd
    def _(g):
        emit_mask(g, NMQ)

    @blkpre.scalar
    def _(s):
        for ti in range(16):
            sl = ti % 2
            if ti >= 1:
                s.wait_ge(S["l_var"], ti)
                s.activation(st((ti - 1) % 2, 9), st((ti - 1) % 2, 8), AF.Sqrt,
                             bias=0.0, scale=1.0).then_inc(S["l_sq"], 1)
            for qq in range(4):
                xq = 4 * ti + qq
                s.wait_ge(S["l_x%d" % (xq % 6)], 16 * (xq // 6 + 1))
                xa = x_t.ap()[:, (xq % 6) * 1024:(xq % 6 + 1) * 1024]
                _w(s, S["l_xq"], 8 * ti + 2 * qq)
                s.activation(xsq.ap(), xa, AF.Identity, bias=0.0, scale=1.0,
                             accum_out=st(sl, qq)).then_inc(S["l_xq"], 1)
                s.wait_ge(S["l_xq"], 8 * ti + 2 * qq + 1)
                s.activation(xsq.ap(), xa, AF.Square,
                             accum_out=st(sl, 4 + qq)).then_inc(S["l_xq"], 1)
            s.activation(st(sl, 9), st(sl, 0), AF.Identity, bias=0.0,
                         scale=1.0).then_inc(S["l_ssq"], 1)
        s.wait_ge(S["l_var"], 16)
        s.activation(st(15 % 2, 9), st(15 % 2, 8), AF.Sqrt, bias=0.0, scale=1.0)\
            .then_inc(S["l_sq"], 1)

    fc = [0]

    mrq = [0]

    def emit_reduce(v, q):
        v.wait_ge(S["m_mk%d" % (q % 5)], 16 * (q // 5 + 1))
        mt, qq = q // 4, q % 4
        with nc.allow_low_precision(reason="0/1 mask 32-sums <=32 exact in bf16"):
            v.tensor_reduce(
                S1k.ap()[:, mt * 128 + qq * 32:mt * 128 + (qq + 1) * 32],
                mk_st.ap()[:, (q % 5) * 1024:(q % 5 + 1) * 1024]
                    .rearrange("p (a b) -> p a b", b=32),
                axis=mybir.AxisListType.X, op=ALU.add,
            ).then_inc(S["m_s1"], 1)
        mrq[0] += 1

    @blkpre.vector
    def _(v):
        def inc(instr):
            fc[0] += 1
            instr.then_inc(S["l_f"], 1)
            return fc[0]

        def wf(n):
            v.wait_ge(S["l_f"], n)

        for ti in range(16):
            sl = ti % 2
            v.wait_ge(S["l_ssq"], ti + 1)
            inc(v.tensor_tensor(st(sl, 10), st(sl, 0), st(sl, 1), op=ALU.add))
            f1 = inc(v.tensor_tensor(st(sl, 11), st(sl, 2), st(sl, 3), op=ALU.add))
            wf(f1)
            f2 = inc(v.tensor_tensor(st(sl, 12), st(sl, 10), st(sl, 11), op=ALU.add))
            wf(f2)
            inc(v.tensor_tensor(st(sl, 10), st(sl, 4), st(sl, 5), op=ALU.add))
            f3 = inc(v.tensor_tensor(st(sl, 11), st(sl, 6), st(sl, 7), op=ALU.add))
            wf(f3)
            f4 = inc(v.tensor_tensor(st(sl, 13), st(sl, 10), st(sl, 11), op=ALU.add))
            wf(f4)
            inc(v.tensor_scalar(st(sl, 14), st(sl, 12), 1.0 / D, None, op0=ALU.mult))
            f5 = inc(v.tensor_scalar(st(sl, 15), st(sl, 13), 1.0 / D, EPS,
                                     op0=ALU.mult, op1=ALU.add))
            wf(f5)
            f6 = inc(v.tensor_tensor(st(sl, 10), st(sl, 14), st(sl, 14), op=ALU.mult))
            wf(f6)
            v.tensor_tensor(st(sl, 8), st(sl, 15), st(sl, 10), op=ALU.subtract)\
                .then_inc(S["l_var"], 1)
            v.wait_ge(S["l_sq"], ti + 1)
            f7 = inc(v.reciprocal(st(sl, 10), st(sl, 9)))
            wf(f7)
            for qq in range(4):
                u = 4 * ti + qq
                _w(v, S["l_xt"], 2 * u - 6)
                v.tensor_scalar(
                    xn_tm.ap()[:, (u % 4) * 1024:(u % 4 + 1) * 1024],
                    x_t.ap()[:, (u % 6) * 1024:(u % 6 + 1) * 1024],
                    st(sl, 14), st(sl, 10),
                    op0=ALU.subtract, op1=ALU.mult,
                ).then_inc(S["l_app"], 1)
            for qq in range(4):
                for sub in range(2):
                    gix = 2 * (4 * ti + qq) + sub
                    v.wait_ge(S["l_xt"], gix + 1)
                    kc0 = 8 * qq + 4 * sub
                    v.tensor_copy(
                        xn_fm.ap().rearrange("p (kc t) -> p kc t", t=T)
                            [:, kc0:kc0 + 4, ti * 128:(ti + 1) * 128],
                        XT_ps[gix % 2].ap().rearrange("p (a b) -> p a b", a=4),
                    ).then_inc(S["l_xev"], 1)
            if DO_MASK:
                for _ in range(4):
                    emit_reduce(v, mrq[0])
                if ti == 9:
                    emit_ST(v, 0)
                if ti == 14:
                    emit_ST(v, 1)
        for q in range(mrq[0], NMQ if DO_MASK else 0):
            emit_reduce(v, q)

    @blkpre.tensor
    def _(t):
        t.wait_ge(S["p_ld"], 16 * 6)
        t.wait_ge(S["p_bm"], 16)
        for u in range(NXQ):
            ti, qq = u // 4, u % 4
            for sub in range(2):
                gix = 2 * u + sub
                t.wait_ge(S["l_app"], u + 1)
                _w(t, S["l_xev"], gix - 1)
                for j in range(4):
                    mm = t.transpose(
                        XT_ps[gix % 2].ap()[:, j * 128:(j + 1) * 128],
                        xn_tm.ap()[:, (u % 4) * 1024 + (4 * sub + j) * 128:
                                   (u % 4) * 1024 + (4 * sub + j + 1) * 128],
                        idbf_sb.ap(),
                    )
                mm.then_inc(S["l_xt"], 1)
            if DO_MASK and u == 31:
                emit_strT(t, 0)
            if DO_MASK and u == 51:
                emit_strT(t, 1)
        t.wait_ge(S["l_xev"], 2 * NXQ)
        for gi in range(5 if DO_S1 else 0):
            _w(t, S["g_ev"], 2 * gi)
            for kc in range(KC):
                u = gi * 8 + kc // 4
                if kc % 4 == 0:
                    t.wait_ge(S["g_rdy"], u + 1)
                for th in range(TH):
                    unit = 2 * gi + th
                    for j in range(2):
                        mm = t.matmul(
                            gen_ps[unit % 2].ap()[:, j * 512:(j + 1) * 512],
                            genring.ap()[:, (u % 3) * 512 + (kc % 4) * 128:
                                         (u % 3) * 512 + (kc % 4 + 1) * 128],
                            xn_fm.ap()[:, kc * T + th * THW + j * 512:
                                       kc * T + th * THW + (j + 1) * 512],
                            start=(kc == 0), stop=(kc == KC - 1),
                        )
                if kc % 4 == 3 and kc != KC - 1:
                    mm.then_inc(S["g_use"], 1)
            mm.then_inc(S["g_acc"], 2)

    ln.close()
    qkh = ExitStack()
    genst = qkh.enter_context(nc.sbuf_tensor("genst", [128, 4 * 512], F32))
    q_fm = qkh.enter_context(nc.sbuf_tensor("q_fm", [128, 2 * T], BF16))
    k_fm = qkh.enter_context(nc.sbuf_tensor("k_fm", [128, 2 * T], BF16))
    h_fm = qkh.enter_context(nc.sbuf_tensor("h_fm", [128, T], BF16))
    eselrg = qkh.enter_context(nc.sbuf_tensor("eselrg", [128, 4 * 512], BF16))
    scfm_t = k_fm
    gen_dst = [q_fm.ap()[:, 0:T], q_fm.ap()[:, T:2 * T],
               k_fm.ap()[:, 0:T], k_fm.ap()[:, T:2 * T], h_fm.ap()]
    wfmT = q_fm.ap()[0:8, 0:T]

    @blkpre.sync
    def _(sync):
        if DO_S1:
            sync.wait_ge(S["l_app"], NXQ)
        for u in range(NGS if DO_S1 else 0):
            _w(sync, S["g_rdy"], u - 3)
            gi, s4 = u // 8, u % 8
            sync.dma_start(
                genst.ap()[:, (u % 4) * 512:(u % 4 + 1) * 512],
                _ap(qkh_d, (gi * KC + 4 * s4) * 128 * 128,
                    [[128, 128], [128 * 128, 4], [1, 128]]),
            ).then_inc(S["gst%d" % (u % 4)], 16)

    @blkpre.scalar
    def _(s):
        def cast(u):
            c = u - 3
            if c >= 0:
                if c % 8 == 7:
                    s.wait_ge(S["g_acc"], 2 * (c // 8) + 2)
                else:
                    s.wait_ge(S["g_use"], use_count(c))
            s.wait_ge(S["gst%d" % (u % 4)], 16 * (u // 4 + 1))
            s.activation(
                genring.ap()[:, (u % 3) * 512:(u % 3 + 1) * 512],
                genst.ap()[:, (u % 4) * 512:(u % 4 + 1) * 512],
                AF.Identity, bias=0.0, scale=1.0,
            ).then_inc(S["g_rdy"], 1)

        def evac(gi):
            for th in range(TH):
                unit = 2 * gi + th
                s.wait_ge(S["g_acc"], 2 * gi + 2)
                s.activation(
                    gen_dst[gi][:, th * THW:(th + 1) * THW],
                    gen_ps[unit % 2].ap(),
                    AF.Identity, bias=0.0, scale=1.0,
                ).then_inc(S["g_ev"], 1)

        if DO_S1:
            for u in range(10):
                cast(u)
            for gi in range(5):
                evac(gi)
                for u in range(8 * gi + 10, min(8 * gi + 18, NGS)):
                    cast(u)

    prp.close()
    pr2 = ExitStack()
    sc_ps = pr2.enter_context(nc.psum_tensor("sc_ps", [8, T], F32))
    st_ps = pr2.enter_context(nc.psum_tensor("st_ps", [128, 128], F32))
    wt_ps = pr2.enter_context(nc.psum_tensor("wt_ps", [8, T], BF16))

    @blkpre.tensor
    def _(t):
        if not DO_S1:
            return
        t.wait_ge(S["g_ev"], 10)
        t.wait_ge(S["g_qk"], 2)
        for ch in range(2):
            for j in range(4):
                mm = t.matmul(
                    sc_ps.ap()[:, j * 512:(j + 1) * 512],
                    sel_sb.ap()[:, ch * 8:(ch + 1) * 8],
                    q_fm.ap()[:, ch * T + j * 512:ch * T + (j + 1) * 512],
                    start=(ch == 0), stop=(ch == 1),
                )
        mm.then_inc(S["g_sc"], 1)
        if DO_MASK:
            emit_strT(t, 2)
            emit_strT(t, 3)
        t.wait_ge(S["g_scev"], 1)
        for tg in range(16):
            mm = t.transpose(
                st_ps.ap()[:, tg * 8:(tg + 1) * 8],
                scfm_t.ap().bitcast(F32)[0:8, 0:T][:, tg * 128:(tg + 1) * 128],
                id32_sb.ap()[0:8, 0:8],
            )
        mm.then_inc(S["g_str"], 1)
        if DO_MASK:
            for g8 in range(4, 8):
                emit_strT(t, g8)
        t.wait_ge(S["g_wx"], 1)
        for tg in range(16):
            mm = t.transpose(
                wt_ps.ap()[:, tg * 128:(tg + 1) * 128],
                wexp_tm[:, tg * 8:(tg + 1) * 8],
                idbf_sb.ap(),
            )
        mm.then_inc(S["g_wt"], 1)

    @blkpre.scalar
    def _(s):
        if not DO_S1:
            return
        s.wait_ge(S["g_sc"], 1)
        s.activation(scfm_t.ap().bitcast(F32)[0:8, 0:T], sc_ps.ap(), AF.Identity, bias=0.0, scale=1.0)\
            .then_inc(S["g_scev"], 1)
        s.wait_ge(S["g_sub"], 1)
        s.activation(exp_t, sc_tm, AF.Exp, bias=0.0, scale=ISQ32)\
            .then_inc(S["g_exp"], 1)

    @blkpre.vector
    def _(v):
        def inc(instr):
            fc[0] += 1
            instr.then_inc(S["l_f"], 1)
            return fc[0]

        def wf(n):
            v.wait_ge(S["l_f"], n)

        if not DO_S1:
            return
        v.wait_ge(S["g_ev"], 8)
        for ch in range(2):
            v.tensor_tensor(
                q_fm.ap()[:, ch * T:(ch + 1) * T],
                q_fm.ap()[:, ch * T:(ch + 1) * T],
                k_fm.ap()[:, ch * T:(ch + 1) * T],
                op=ALU.mult,
            ).then_inc(S["g_qk"], 1)
        if DO_MASK:
            emit_ST(v, 2)
            emit_ST(v, 3)
        v.wait_ge(S["g_str"], 1)
        g1 = inc(v.tensor_copy(sc_tm, st_ps.ap()))
        wf(g1)
        g2 = inc(v.tensor_reduce(m_t, sc_tm.rearrange("p (a b) -> p a b", b=8),
                                 axis=mybir.AxisListType.X, op=ALU.max))
        wf(g2)
        v.tensor_tensor(
            sc_tm.rearrange("p (a b) -> p a b", b=8),
            sc_tm.rearrange("p (a b) -> p a b", b=8),
            m_t.broadcast_to((128, 16, 8)), op=ALU.subtract,
        ).then_inc(S["g_sub"], 1)
        if DO_MASK:
            emit_ST(v, 4)
            emit_ST(v, 5)
        v.wait_ge(S["g_exp"], 1)
        g3 = inc(v.tensor_reduce(z_t, exp_t.rearrange("p (a b) -> p a b", b=8),
                                 axis=mybir.AxisListType.X, op=ALU.add))
        wf(g3)
        g4 = inc(v.reciprocal(zr_t, z_t))
        wf(g4)
        if DO_MASK:
            emit_ST(v, 6)
            emit_ST(v, 7)
        v.tensor_tensor(
            wexp_tm.rearrange("p (a b) -> p a b", b=8),
            exp_t.rearrange("p (a b) -> p a b", b=8),
            zr_t.broadcast_to((128, 16, 8)), op=ALU.mult,
        ).then_inc(S["g_wx"], 1)
        v.wait_ge(S["g_wt"], 1)
        v.tensor_copy(wfmT, wt_ps.ap()).then_inc(S["g_wtv"], 1)

    pr2.close()
    pr3 = ExitStack()
    e_ps = pr3.enter_context(nc.psum_tensor("e_ps", [128, THW], F32))
    we_ps = pr3.enter_context(nc.psum_tensor("we_ps", [128, T], F32))

    def emit_e16(t):
        for j in range(4):
            mm = t.matmul(
                we_ps.ap()[:, j * 512:(j + 1) * 512],
                E16_sb.ap(),
                wfmT[:, j * 512:(j + 1) * 512],
                start=True, stop=True,
            )
        mm.then_inc(S["g_we"], 1)

    @blkpre.tensor
    def _(t):
        if DO_S1:
            t.wait_ge(S["g_wtv"], 1)
            emit_e16(t)
        if not DO_MASK:
            return
        t.wait_ge(S["m_bm"], 1)
        t.wait_ge(S["g_ev"], 10)
        if DO_S1:
            t.wait_ge(S["g_wtv"], 1)
        for g8 in range(KC // 8):
            for k in (2 * g8, 2 * g8 + 1):
                t.wait_ge(S["p_e%d" % (k % 4)], 16 * (k // 4 + 1))
            _w(t, S["m_bev"], g8)
            for j in range(8):
                slot = (2 * g8 + j // 4) % 4
                mm = t.matmul(
                    e_ps.ap()[:, j * 128:(j + 1) * 128],
                    eselrg.ap()[:, slot * 512 + (j % 4) * 128:
                                slot * 512 + (j % 4 + 1) * 128],
                    bmT_sb.ap(),
                    start=True, stop=True,
                )
            mm.then_inc(S["m_bexp"], 1)

    @blkpre.vector
    def _(v):
        def inc(instr):
            fc[0] += 1
            instr.then_inc(S["l_f"], 1)
            return fc[0]

        def wf(n):
            v.wait_ge(S["l_f"], n)

        if DO_S1:
            v.wait_ge(S["g_we"], 1)
            v.wait_ge(S["g_ev"], 10)
        for j in range(4 if DO_S1 else 0):
            v.tensor_tensor(
                wh_fm.ap()[:, j * 512:(j + 1) * 512],
                h_fm.ap()[:, j * 512:(j + 1) * 512],
                we_ps.ap()[:, j * 512:(j + 1) * 512],
                op=ALU.mult,
            ).then_inc(S["g_wh"], 1)
        if DO_MASK:
            v.wait_ge(S["m_trv"], 8)
            fz = inc(v.tensor_scalar(bmT_sb.ap(), ST.ap(), 0.0, None, op0=ALU.is_gt))
            wf(fz)
            v.nop().then_inc(S["m_bm"], 1)
        for g8 in range(KC // 8 if DO_MASK else 0):
            v.wait_ge(S["m_bexp"], g8 + 1)
            v.tensor_copy(
                bmE_pT.ap()[:, g8 * 1024:(g8 + 1) * 1024],
                e_ps.ap(),
            ).then_inc(S["m_bev"], 1)
        v.wait_ge(S["p_ld"], 16 * 6)
        v.tensor_tensor(biassum.ap(), biassum.ap(), biasB_sb.ap(), op=ALU.add)

    @blkpre.sync
    def _(sync):
        if DO_MASK:
            sync.wait_ge(S["l_app"], NXQ)
        for k in range(8 if DO_MASK else 0):
            _w(sync, S["m_bexp"], k // 2 - 1)
            sync.dma_start(eselrg.ap()[:, (k % 4) * 512:(k % 4 + 1) * 512],
                           Esel_d[:, k * 512:(k + 1) * 512])\
                .then_inc(S["p_e%d" % (k % 4)], 16)

    blkpre.__exit__(None, None, None)
    pr3.close()
    prpT.close()
    qkh.close()
    pre.close()

    mn = ExitStack()
    wm_ring = mn.enter_context(nc.sbuf_tensor("wm_ring", [128, 2 * D], BF16))
    WT_ring = mn.enter_context(nc.sbuf_tensor("WT_ring", [128, 2 * D], BF16))
    x_res = mn.enter_context(nc.sbuf_tensor("x_res", [128, 4 * THW], F32))
    stuff = mn.enter_context(nc.sbuf_tensor("stuff", [128, 2 * THW], F32))
    mnp = ExitStack()
    ACC0 = mnp.enter_context(nc.psum_tensor("ACC0", [128, THW], F32))
    ACC1 = mnp.enter_context(nc.psum_tensor("ACC1", [128, THW], F32))
    T_ps0 = mnp.enter_context(nc.psum_tensor("T_ps0", [128, THW], F32))
    T_ps1 = mnp.enter_context(nc.psum_tensor("T_ps1", [128, THW], F32))
    ACC = [ACC0, ACC1]
    T_ps = [T_ps0, T_ps1]

    def x_res_ap(th, nc_i):
        return _ap(x_d, (th * THW) * D + nc_i * 128,
                   [[D, 128], [128 * D, 8], [1, 128]])

    def out_ap(th, nc_i, j):
        return _ap(out_d, (th * THW + j * 512) * D + nc_i * 128,
                   [[D, 128], [128 * D, 4], [1, 128]])

    def emit_tr(t, m):
        for th in range(TH):
            t.wait_ge(S["a_stuff"], 2 * m + th + 1)
            _w(t, S["a_add"], (4 * (m - 1) + 2 * th + 2) if m >= 1 else 0)
            for j8 in range(8):
                mm = t.transpose(
                    T_ps[th].ap()[:, j8 * 128:(j8 + 1) * 128],
                    stuff.ap()[:, th * THW + j8 * 128:th * THW + (j8 + 1) * 128],
                    id32_sb.ap(),
                )
            mm.then_inc(S["a_tr"], 1)

    with nc.Block() as blk:
        @blk.gpsimd
        def _(g):
            for nc_i in range(NC):
                _w(g, S["a_acc"], 2 * (nc_i - 1))
                g.dma_start(
                    wm_ring.ap()[:, (nc_i % 2) * D:(nc_i % 2 + 1) * D],
                    _ap(wTb_d, nc_i * 128 * D, [[D, 128], [1, D]]),
                    max_dma_last_dim=8192,
                ).then_inc(S["a_wld%d" % (nc_i % 2)], 16)
                g.dma_start(
                    WT_ring.ap()[:, (nc_i % 2) * D:(nc_i % 2 + 1) * D],
                    _ap(WTb_d, nc_i * 128 * D, [[D, 128], [1, D]]),
                    max_dma_last_dim=8192,
                ).then_inc(S["a_Wld%d" % (nc_i % 2)], 16)

        @blk.vector
        def _(v):
            for i in range(NC + 1):
                if i < NC:
                    v.wait_ge(S["a_wld%d" % (i % 2)], 16 * (i // 2 + 1))
                    w_view = wm_ring.ap()[:, (i % 2) * D:(i % 2 + 1) * D]\
                        .rearrange("p (kc q r) -> p kc q r", kc=KC, q=4)
                    v.tensor_tensor(
                        w_view, w_view,
                        bmE_pT.ap().rearrange("p (kc nb) -> p kc nb", nb=128)
                            [:, :, 4 * i:4 * i + 4].broadcast_to((128, KC, 4, 32)),
                        op=ALU.mult,
                    ).then_inc(S["a_wmask"], 1)
                if i >= 1:
                    nc_i = i - 1
                    for th in range(TH):
                        for j in range(2):
                            xr = x_res.ap()[:, (nc_i % 2) * 2 * THW + th * THW + j * 512:
                                            (nc_i % 2) * 2 * THW + th * THW + (j + 1) * 512]
                            v.wait_ge(S["a_tr"], 2 * nc_i + th + 1)
                            v.wait_ge(S["a_xr%d" % (2 * th + nc_i % 2)],
                                      16 * (nc_i // 2 + 1))
                            v.tensor_tensor(
                                xr,
                                T_ps[th].ap()[:, j * 512:(j + 1) * 512],
                                xr,
                                op=ALU.add,
                            ).then_inc(S["a_add"], 1)

        @blk.tensor
        def _(t):
            for nc_i in range(NC):
                t.wait_ge(S["a_wmask"], nc_i + 1)
                t.wait_ge(S["a_Wld%d" % (nc_i % 2)], 16 * (nc_i // 2 + 1))
                sl = nc_i % 2
                for th in range(TH):
                    if th == 1 and nc_i >= 1:
                        emit_tr(t, nc_i - 1)
                    _w(t, S["a_stuff"], 2 * (nc_i - 1) + th + 1)
                    for kc in range(KC):
                        for ring in (wm_ring, WT_ring):
                            for j in range(2):
                                mm = t.matmul(
                                    ACC[th].ap()[:, j * 512:(j + 1) * 512],
                                    ring.ap()[:, sl * D + kc * 128:sl * D + (kc + 1) * 128],
                                    xn_fm.ap()[:, kc * T + th * THW + j * 512:
                                               kc * T + th * THW + (j + 1) * 512],
                                    start=(kc == 0 and ring is wm_ring),
                                    stop=False,
                                )
                    for j in range(2):
                        mm = t.matmul(
                            ACC[th].ap()[:, j * 512:(j + 1) * 512],
                            BmT_sb.ap()[:, nc_i * 128:(nc_i + 1) * 128],
                            wh_fm.ap()[:, th * THW + j * 512:th * THW + (j + 1) * 512],
                            start=False, stop=True,
                        )
                    mm.then_inc(S["a_acc"], 1)
            emit_tr(t, NC - 1)

        @blk.scalar
        def _(s):
            for nc_i in range(NC):
                for th in range(TH):
                    s.wait_ge(S["a_acc"], 2 * nc_i + th + 1)
                    _w(s, S["a_tr"], 2 * (nc_i - 1) + th + 1)
                    s.activation(
                        stuff.ap()[:, th * THW:(th + 1) * THW],
                        ACC[th].ap(),
                        AF.Identity,
                        bias=biassum.ap()[:, nc_i:nc_i + 1],
                        scale=1.0,
                    ).then_inc(S["a_stuff"], 1)

        @blk.sync
        def _(sync):
            for i in range(NC + 1):
                if i < NC:
                    for th in range(TH):
                        if i >= 2:
                            sync.wait_ge(S["a_st%d" % (2 * th)], 16 * (i - 1))
                            sync.wait_ge(S["a_st%d" % (2 * th + 1)], 16 * (i - 1))
                        sync.dma_start(
                            x_res.ap()[:, (i % 2) * 2 * THW + th * THW:
                                       (i % 2) * 2 * THW + (th + 1) * THW],
                            x_res_ap(th, i),
                        ).then_inc(S["a_xr%d" % (2 * th + i % 2)], 16)
                if i >= 1:
                    nc_i = i - 1
                    for th in range(TH):
                        for j in range(2):
                            gidx = 4 * nc_i + 2 * th + j
                            sj = 2 * th + j
                            sync.wait_ge(S["a_add"], gidx + 1)
                            sync.dma_start(
                                out_ap(th, nc_i, j),
                                x_res.ap()[:, (nc_i % 2) * 2 * THW + th * THW + j * 512:
                                           (nc_i % 2) * 2 * THW + th * THW + (j + 1) * 512],
                            ).then_inc(S["a_st%d" % sj], 16)
            for sj in range(4):
                sync.wait_ge(S["a_st%d" % sj], 16 * 32)

    mnp.close()
    mn.close()
    es.close()
    return nc


def _host_constants():
    p = np.arange(128)
    blk = p // 32
    E3 = np.zeros((128, 32, 128), np.float32)
    ET3 = np.zeros((128, 32, 128), np.float32)
    for kc in range(32):
        E3[4 * kc + blk, kc, p] = 1.0
        ET3[p, kc, 4 * kc + blk] = 1.0
    Esel = E3.reshape(128, 4096).astype(ml_dtypes.bfloat16)
    ETsel = ET3.reshape(128, 4096).astype(ml_dtypes.bfloat16)
    sel = np.zeros((128, 16), np.float32)
    for ch in range(2):
        sel[p, ch * 8 + ch * 4 + blk] = 1.0
    selqk = sel.astype(ml_dtypes.bfloat16)
    E16 = np.zeros((8, 128), np.float32)
    E16[np.arange(128) // 16, np.arange(128)] = 1.0
    E16 = E16.astype(ml_dtypes.bfloat16)
    id32 = np.eye(128, dtype=np.float32)
    idbf = np.eye(128).astype(ml_dtypes.bfloat16)
    return Esel, ETsel, selqk, E16, id32, idbf


def kernel(L0_attn_data, L0_attn_mask, L0_attn_weight, L0_attn_bias,
           L0_ffn_W, L0_ffn_b, L0_ffn_Wq, L0_ffn_Wk, L0_ffn_A, L0_ffn_Bm):
    x_full = np.ascontiguousarray(np.asarray(L0_attn_data, np.float32)).reshape(16384, 4096)
    mask = np.ascontiguousarray(np.asarray(L0_attn_mask, np.int32))
    w = np.asarray(L0_attn_weight, np.float32)
    W = np.asarray(L0_ffn_W, np.float32)

    def blocked_T(m):
        return np.ascontiguousarray(
            m.reshape(32, 128, 32, 128).transpose(0, 3, 2, 1).reshape(32, 128, 4096))

    wTb = blocked_T(w)
    WTb = blocked_T(W)

    Wq = np.asarray(L0_ffn_Wq, np.float32)
    Wk = np.asarray(L0_ffn_Wk, np.float32)
    A = np.asarray(L0_ffn_A, np.float32).reshape(128, 4096)
    qkh = np.concatenate([Wq.T, Wk.T, A.T], axis=1)
    qkhTb = np.ascontiguousarray(
        qkh.reshape(32, 128, 5, 128).transpose(2, 0, 1, 3))

    BmT = np.ascontiguousarray(
        np.asarray(L0_ffn_Bm, np.float32).transpose(0, 2, 1).reshape(128, 4096))
    biasA = np.ascontiguousarray(np.asarray(L0_attn_bias, np.float32).reshape(32, 128).T)
    biasB = np.ascontiguousarray(np.asarray(L0_ffn_b, np.float32).reshape(32, 128).T)

    Esel, _ETsel_unused, selqk, E16, id32, idbf = _host_constants()

    nc = build_nc()
    shared = dict(mask=mask, wTb=wTb, WTb=WTb, qkhTb=qkhTb, BmT=BmT,
                  biasA=biasA, biasB=biasB, Esel=Esel,
                  selqk=selqk, E16=E16, id32=id32, idbf=idbf)
    in_maps = [dict(x=np.ascontiguousarray(x_full[c * 2048:(c + 1) * 2048]), **shared)
               for c in range(NCORE)]
    res = run_bass_kernel_spmd(nc, in_maps, core_ids=list(range(NCORE)))
    out = np.concatenate([np.asarray(res.results[c]["out"]) for c in range(NCORE)], axis=0)
    return out.reshape(8, 2048, 4096)
